# version 03
# baseline (speedup 1.0000x reference)
import contextlib
import numpy as np
import ml_dtypes
import concourse.bass as bass
import concourse.mybir as mybir
from concourse.bass_utils import run_bass_kernel_spmd

F32 = mybir.dt.float32
BF16 = mybir.dt.bfloat16
AF = mybir.ActivationFunctionType
ALU = mybir.AluOpType
AXX = mybir.AxisListType.X

NC = 8
TOK = 1024
HALO = 3
TW = TOK + HALO
DM = 4096
DI = 8192
H, P, GQ, NS, CH, NCH = 128, 64, 8, 128, 128, 8
EPS = 1e-5
NDT = DM // 128
BF = ml_dtypes.bfloat16

FT_HID0, FT_B0, FT_C0, FT_DT = 64, 128, 136, 144
G_HBC = [(0, 344), (344, 344), (688, 339)]
G_GATE = [(3, 512), (515, 512)]
P0G = [(0, 512), (512, 512), (1024, 3)]
RG8 = [[0, 1, 2, 3, 4, 5, 6, 7]]


def _tile_lhsT(wT):
    K, M = wT.shape
    Kt, Mt = K // 128, M // 128
    r = wT.reshape(Kt, 128, Mt, 128).transpose(2, 1, 0, 3)
    return np.ascontiguousarray(r.reshape(Mt * 128, K))


def _host_prep(inputs):
    d = {}
    d["win"] = _tile_lhsT(np.ascontiguousarray(inputs["L0_attn_in_proj_w"].T))
    d["wout"] = _tile_lhsT(np.ascontiguousarray(inputs["L0_attn_out_proj_w"].T))
    d["wbase"] = _tile_lhsT(np.ascontiguousarray(inputs["L0_ffn_base_w"].T))
    BT = np.ascontiguousarray(inputs["L0_ffn_B"].transpose(0, 2, 1))
    Bp = np.concatenate([BT[0::2], BT[1::2]], axis=1)
    d["wbp"] = np.ascontiguousarray(
        Bp.reshape(8, 128, 32, 128).transpose(2, 1, 0, 3).reshape(32 * 128, 8 * 128))
    AT = np.ascontiguousarray(inputs["L0_ffn_A"].T)
    d["wat"] = np.ascontiguousarray(AT.reshape(32, 128, 64).transpose(1, 0, 2).reshape(128, 32 * 64))
    RT = np.ascontiguousarray(inputs["L0_ffn_router_w"].T)
    d["wrout"] = np.ascontiguousarray(RT.reshape(32, 128, 16).transpose(1, 0, 2).reshape(128, 32 * 16))
    d["nwcol"] = np.ascontiguousarray(inputs["L0_attn_norm_w"].reshape(32, 128).T)
    d["cwt"] = np.ascontiguousarray(
        inputs["L0_attn_conv_w"].reshape(80, 128, 4).transpose(1, 0, 2).reshape(128, 320))
    d["cbt"] = np.ascontiguousarray(inputs["L0_attn_conv_b"].reshape(80, 128).T)
    d["dtb"] = np.ascontiguousarray(inputs["L0_attn_dt_bias"].reshape(128, 1))
    d["alog"] = np.ascontiguousarray(inputs["L0_attn_A_log"].reshape(128, 1))
    d["dpc"] = np.ascontiguousarray(np.repeat(inputs["L0_attn_Dp"], 64).reshape(64, 128).T)
    d["gnc"] = np.ascontiguousarray(inputs["L0_attn_gnorm_w"].reshape(64, 128).T)
    d["bbc"] = np.ascontiguousarray(inputs["L0_ffn_base_b"].reshape(32, 128).T)
    d["idn"] = np.eye(128, dtype=np.float32)
    d["trim"] = np.triu(np.ones((128, 128), dtype=np.float32))
    x = inputs["L0_attn_x"]
    d["xt"], d["gmt"], d["mcol"] = [], [], []
    for k in range(NC):
        b, q = k // 4, k % 4
        t0 = 1024 * q
        xs = np.zeros((TW, DM), np.float32)
        lo = max(0, t0 - HALO)
        xs[HALO - (t0 - lo):] = x[b, lo:t0 + TOK]
        xT = np.ascontiguousarray(xs.T)
        d["xt"].append(np.ascontiguousarray(
            xT.reshape(32, 128, TW).transpose(1, 0, 2).reshape(128, 32 * TW)))
        g = np.zeros((8, 8), np.float32)
        m = np.zeros((8, 1), np.float32)
        for j in range(8):
            if j // 4 == b and j % 4 < q:
                m[j] = 1.0
                for i in range(8):
                    if i // 4 == b and (j % 4) < (i % 4) < q:
                        g[j, i] = 1.0
        gT = np.zeros((8, 128), np.float32)
        gT[:, 0:8] = g.T
        d["gmt"].append(gT)
        d["mcol"].append(m)
    return d


class Ctx:
    def __init__(self, nc):
        self.nc = nc
        self.dry = True
        self.sems = {}
        self.handles = {}
        self.ms = {}

    def sem(self, name):
        self.sems.setdefault(name, 0)
        return name


NSUB = 4


class Eng:
    def __init__(self, ctx, semname, auto_serial=False, dma_family=None):
        self.ctx = ctx
        self.sn = ctx.sem(semname)
        self.real = None
        self.auto = auto_serial
        self.fam = dma_family
        if dma_family:
            self.subs = [ctx.sem(f"{dma_family}{i}") for i in range(NSUB)]
        self.rr = 0
        self.last_dma = None
        self._last_was_dma = False

    def mark(self, name):
        if self.ctx.dry:
            assert name not in self.ctx.ms, f"dup milestone {name}"
            if self._last_was_dma:
                self.ctx.ms[name] = ("MULTI", tuple(
                    (s, self.ctx.sems[s]) for s in self.subs if self.ctx.sems[s] > 0))
            else:
                self.ctx.ms[name] = (self.sn, self.ctx.sems[self.sn])

    def waitm(self, ms_name):
        if not self.ctx.dry:
            sn, v = self.ctx.ms[ms_name]
            if sn == "MULTI":
                for s, c in v:
                    self.real.wait_ge(self.ctx.handles[s], c)
            else:
                self.real.wait_ge(self.ctx.handles[sn], v)

    def wait_count(self, semname, v):
        if not self.ctx.dry:
            self.real.wait_ge(self.ctx.handles[semname], v)

    def wait_queue_idle(self, eng):
        if not self.ctx.dry:
            for s in eng.subs:
                self.real.wait_ge(self.ctx.handles[s], self.ctx.sems[s])

    def op(self, fn, *args, inc=1, semname=None, **kw):
        sn = semname or self.sn
        if self.auto and not self.ctx.dry and self.ctx.sems[self.sn] > 0 \
                and fn not in ("dma_start",):
            self.real.wait_ge(self.ctx.handles[self.sn], self.ctx.sems[self.sn])
        if inc:
            self.ctx.sems[sn] += inc
        self._last_was_dma = False
        if self.ctx.dry:
            return None
        instr = getattr(self.real, fn)(*args, **kw)
        if inc:
            instr.then_inc(self.ctx.handles[sn], inc)
        return instr

    def dma(self, out, in_, **kw):
        s = self.subs[self.rr % NSUB]
        self.rr += 1
        if self.ctx.sems[s] > 0 and not self.ctx.dry:
            self.real.wait_ge(self.ctx.handles[s], self.ctx.sems[s])
        self.ctx.sems[s] += 16
        self.last_dma = (s, self.ctx.sems[s])
        self._last_was_dma = True
        if self.ctx.dry:
            return None
        instr = self.real.dma_start(out, in_, **kw)
        instr.then_inc(self.ctx.handles[s], 16)
        return instr

    def dmark(self, name):
        self.op("drain", inc=1)
        self.mark(name)


def _run_phase(ctx, nc, sections):
    snap = dict(ctx.sems)
    esnap = {id(p): (p.rr, p.last_dma, p._last_was_dma) for _, p, _ in sections}
    ctx.dry = True
    for _, _, fn in sections:
        fn()
    dry_end = dict(ctx.sems)
    dry_rr = {id(p): p.rr for _, p, _ in sections}
    ctx.sems.clear()
    ctx.sems.update(snap)
    for _, p, _ in sections:
        if id(p) in esnap:
            p.rr, p.last_dma, p._last_was_dma = esnap[id(p)]
    ctx.dry = False
    with nc.Block() as blk:
        for kind, proxy, fn in sections:
            def body(real, p=proxy, f=fn):
                p.real = real
                f()
            getattr(blk, kind)(body)
    assert ctx.sems == dry_end, (
        "count drift: " + str({k: (ctx.sems.get(k), dry_end.get(k))
                               for k in set(ctx.sems) | set(dry_end)
                               if ctx.sems.get(k) != dry_end.get(k)}))
    for _, p, _ in sections:
        assert p.rr == dry_rr[id(p)], f"rr drift {p.sn}"
    ctx.dry = True


NO_SIN = False
FAKE_CC = False
NO_GNORM = False
G_LIM = None


def build(stage=1, debug=False):
    nc = bass.Bass("TRN2", target_bir_lowering=False, debug=False)
    EI, EO = "ExternalInput", "ExternalOutput"
    DBG = EO if debug else "Internal"

    def din(name, shape, dt=F32, kind=EI):
        return nc.dram_tensor(name, shape, dt, kind=kind).ap()

    xt = din("xt", [128, 32 * TW])
    win = din("win", [145 * 128, DM])
    wout = din("wout", [32 * 128, DI])
    wbase = din("wbase", [32 * 128, DM])
    wbp = din("wbp", [32 * 128, 8 * 128])
    wat = din("wat", [128, 32 * 64])
    wrout = din("wrout", [128, 32 * 16])
    nwcol = din("nwcol", [128, 32])
    cwt = din("cwt", [128, 320])
    cbt = din("cbt", [128, 80])
    dtb = din("dtb", [128, 1])
    alog = din("alog", [128, 1])
    dpc = din("dpc", [128, 64])
    gnc = din("gnc", [128, 64])
    bbc = din("bbc", [128, 32])
    idn = din("idn", [128, 128])
    trim = din("trim", [128, 128])
    gmt = din("gmt", [8, 128])
    mcolA = din("mcol", [8, 1])

    hidb = din("hidb", [64 * 128, 1024], BF16, DBG)
    hcb = din("hcb", [8 * 128, 8192], BF16, DBG)
    bcb = din("bcb", [16 * 128, 1024], BF16, DBG)
    gateb = din("gateb", [64 * 128, 1024], BF16, DBG)
    pclb = din("pclb", [64 * 128, 1024], F32, DBG)
    acsb = din("acsb", [128, 1024], F32, DBG)
    dtd = din("dtd", [128, 1024], F32, DBG)
    offsd = din("offsd", [128, 16], F32, "Internal")
    ccin = din("ccin", [128, 8192], F32, "Internal")
    ccin2 = din("ccin2", [1, 128], F32, "Internal")
    ccout2 = din("ccout2", [8, 128], F32, "Internal")
    ccout = din("ccout", [1024, 8192], F32, "Internal")
    wscr = din("wscr", [8, 128], F32, "Internal")
    sind = din("sind", [128, 8192], F32, DBG)
    ybuf = din("ybuf", [64 * 128, 1024], BF16, DBG)
    xab = din("xab", [32 * 128, 1024], F32, DBG)
    outT = nc.dram_tensor("outT", [DM, 1024], F32, kind=EO).ap()

    ctx = Ctx(nc)
    pe = Eng(ctx, "TP")
    dv = Eng(ctx, "TV", auto_serial=True)
    ac = Eng(ctx, "TA", auto_serial=True, dma_family="QA")
    gp = Eng(ctx, "QG", dma_family="QG")
    sy = Eng(ctx, "QS", dma_family="QS")
    ctx.sem("CK")

    with contextlib.ExitStack() as stk:
        for sname in list(ctx.sems):
            ctx.handles[sname] = stk.enter_context(nc.semaphore(sname))

        SB = lambda n, s, t: stk.enter_context(nc.sbuf_tensor(n, s, t))

        ones_bc = SB("ones_bc", [128, 1], BF16)
        ones_bb = SB("ones_bb", [128, 128], BF16)
        ones_fr = SB("ones_fr", [1, 128], F32)
        idbf = SB("idbf", [128, 128], BF16)
        trimask = SB("trimask", [128, 128], BF16)
        idf = SB("idf", [128, 128], F32)
        nwc = SB("nwc", [128, 32], F32)
        cw = SB("cw", [128, 320], F32)
        cbS = SB("cbS", [128, 80], F32)
        dtbias = SB("dtbias", [128, 1], F32)
        negA = SB("negA", [128, 1], F32)
        dpcol = SB("dpcol", [128, 64], F32)
        gncol = SB("gncol", [128, 64], F32)
        bbcol = SB("bbcol", [128, 32], F32)
        epscol = SB("epscol", [128, 1], F32)
        onecol = SB("onecol", [128, 1], F32)
        mrow = SB("mrow", [1, TW], F32)
        rsrow = SB("rsrow", [1, TW], F32)
        sDT = contextlib.ExitStack()
        SBD = lambda n, s, t: sDT.enter_context(nc.sbuf_tensor(n, s, t, side="right"))
        acs_tok = SBD("acs_tok", [128, NCH * 128], F32)
        offs = SBD("offs", [128, 16], F32)
        sDT2 = contextlib.ExitStack()
        SBD2 = lambda n, s, t: sDT2.enter_context(nc.sbuf_tensor(n, s, t, side="right"))

        def c_gp():
            gp.dma(trimask[:], trim)
            gp.dma(idbf[:], idn)
            gp.dma(cw[:], cwt)
            gp.mark("c_qg")

        def c_sy():
            sy.dma(idf[:], idn)
            sy.dma(nwc[:], nwcol)
            sy.dma(cbS[:], cbt)
            sy.dma(dtbias[:], dtb)
            sy.dma(negA[:], alog)
            sy.dma(dpcol[:], dpc)
            sy.dma(gncol[:], gnc)
            sy.dma(bbcol[:], bbc)
            sy.mark("c_qs")

        def c_dv():
            dv.op("memset", ones_bc[:], 1.0)
            dv.op("memset", ones_bb[:], 1.0)
            dv.op("memset", ones_fr[:], 1.0)
            dv.op("memset", epscol[:], EPS)
            dv.op("memset", onecol[:], 1.0)
            dv.mark("c_ones")

        def c_ac():
            ac.waitm("c_qs")
            ac.op("activation", negA[:], negA[:], AF.Exp)
            ac.mark("c_expA")

        def c_dv2():
            dv.waitm("c_expA")
            dv.op("tensor_scalar_mul", negA[:], negA[:], -1.0)
            dv.mark("c_negA")

        _run_phase(ctx, nc, [
            ("gpsimd", gp, c_gp), ("sync", sy, c_sy), ("vector", dv, c_dv),
            ("scalar", ac, c_ac), ("vector", dv, c_dv2),
        ])

        sSin = contextlib.ExitStack()
        S_in_bf = sSin.enter_context(nc.sbuf_tensor("S_in_bf", [128, 8192], BF16, side="right"))
        sH = contextlib.ExitStack()
        hT = sH.enter_context(nc.sbuf_tensor("hT", [128, 32 * TW], BF16, side="right"))
        dt_tok = SBD2("dt_tok", [128, NCH * 128], BF16)
        dc_all = SBD2("dc_all", [128, NCH * 128], BF16)
        expa = SBD2("expa", [128, NCH * 128], F32)

        with contextlib.ExitStack() as s0:
            SB0 = lambda n, s, t: s0.enter_context(nc.sbuf_tensor(n, s, t))
            PS0 = lambda n, s, t: s0.enter_context(nc.psum_tensor(n, s, t))
            sqt = SB0("sqt", [128, TW], BF16)
            mbb = SB0("mbb", [128, TW], BF16)
            csbuf = SB0("csbuf", [128, TW], BF16)
            vf = SB0("vf", [128, TW], F32)
            wf = SB0("wf", [128, TW], F32)
            t2 = SB0("t2", [128, TW], F32)
            ps_r = [PS0(f"ps_r{i}", [128, 512], F32) for i in range(3)]

            def p0_gp():
                gp.dma(hT[:], xt)
                gp.mark("p0_x")

            def p0_pe():
                pe.waitm("p0_x")
                pe.waitm("c_ones")
                for gi, (c0, n) in enumerate(P0G):
                    for dt in range(NDT):
                        last = (gi == 2 and dt == NDT - 1)
                        pe.op("matmul", ps_r[gi][:, 0:n], ones_bb[:],
                              hT[:, dt * TW + c0: dt * TW + c0 + n],
                              start=(dt == 0), stop=(dt == NDT - 1), inc=1 if last else 0)
                pe.mark("p0_sum")
                for dt in range(NDT):
                    pe.waitm(f"p0_sq{dt}")
                    if dt == 0:
                        pe.waitm("p0_m")
                    for gi, (c0, n) in enumerate(P0G):
                        pe.op("matmul", ps_r[gi][:, 0:n], ones_bb[:],
                              sqt[:, c0:c0 + n],
                              start=(dt == 0), stop=(dt == NDT - 1),
                              inc=1 if gi == 2 else 0)
                    pe.mark(f"p0_qmm{dt}")
                pe.mark("p0_sumsq")
                for dt in range(NDT):
                    pe.waitm(f"p0_sq2_{dt}")
                    if dt == 0:
                        pe.waitm("p0_var")
                    for gi, (c0, n) in enumerate(P0G):
                        pe.op("matmul", ps_r[gi][:, 0:n], ones_bb[:],
                              sqt[:, c0:c0 + n],
                              start=(dt == 0), stop=(dt == NDT - 1),
                              inc=1 if gi == 2 else 0)
                    pe.mark(f"p0_q2mm{dt}")
                pe.mark("p0_sumsq2")

            def p0_dv():
                dv.waitm("p0_sum")
                for gi, (c0, n) in enumerate(P0G):
                    dv.op("tensor_scalar_mul", vf[:, c0:c0 + n], ps_r[gi][:, 0:n], 1.0 / DM)
                dv.op("tensor_copy", mbb[:], vf[:])
                dv.op("tensor_copy", mrow[0:1, :], vf[0:1, :])
                dv.mark("p0_m")
                dv.waitm("p0_sumsq")
                for gi, (c0, n) in enumerate(P0G):
                    dv.op("tensor_scalar_mul", wf[:, c0:c0 + n], ps_r[gi][:, 0:n], 1.0 / DM)
                dv.op("tensor_mul", t2[:], vf[:], vf[:])
                dv.op("tensor_sub", wf[:], wf[:], t2[:])
                dv.mark("p0_var")
                dv.waitm(f"p0_sq{NDT-1}")
                for dt in range(NDT):
                    dv.op("tensor_sub", hT[:, dt * TW:(dt + 1) * TW],
                          hT[:, dt * TW:(dt + 1) * TW], mbb[:])
                dv.mark("p0_h1")
                dv.waitm("p0_rstd")
                dv.op("tensor_copy", rsrow[0:1, :], wf[0:1, :])
                dv.mark("p0_rsrow")
                dv.waitm("p0_sumsq2")
                for gi, (c0, n) in enumerate(P0G):
                    dv.op("tensor_scalar_mul", t2[:, c0:c0 + n], ps_r[gi][:, 0:n], 1.0 / DM)
                dv.op("tensor_mul", t2[:], t2[:], wf[:])
                dv.op("tensor_mul", t2[:], t2[:], wf[:])
                dv.mark("p0_rms2")
                dv.waitm("p0_rs2")
                dv.op("tensor_mul", t2[:], t2[:], wf[:])
                dv.op("tensor_copy", csbuf[:], t2[:])
                dv.mark("p0_cs")
                dv.waitm(f"p0_sq2_{NDT-1}")
                for dt in range(NDT):
                    dv.op("scalar_tensor_tensor", hT[:, dt * TW:(dt + 1) * TW],
                          hT[:, dt * TW:(dt + 1) * TW], nwc[:, dt:dt + 1], csbuf[:],
                          ALU.mult, ALU.mult)
                dv.mark("p0_hT")

            def p0_ac():
                for dt in range(NDT):
                    if dt == 0:
                        ac.waitm("p0_x")
                    else:
                        ac.waitm(f"p0_qmm{dt-1}")
                    ac.op("activation", sqt[:], hT[:, dt * TW:(dt + 1) * TW], AF.Square)
                    ac.mark(f"p0_sq{dt}")
                ac.waitm("p0_var")
                ac.op("activation", wf[:], wf[:], AF.Ln, bias=epscol[:])
                ac.op("activation", wf[:], wf[:], AF.Exp, scale=-0.5)
                ac.mark("p0_rstd")
                for dt in range(NDT):
                    if dt == 0:
                        ac.waitm("p0_h1")
                    else:
                        ac.waitm(f"p0_q2mm{dt-1}")
                    ac.op("activation", sqt[:], hT[:, dt * TW:(dt + 1) * TW], AF.Square)
                    ac.mark(f"p0_sq2_{dt}")
                ac.waitm("p0_rms2")
                ac.op("activation", t2[:], t2[:], AF.Ln, bias=epscol[:])
                ac.op("activation", t2[:], t2[:], AF.Exp, scale=-0.5)
                ac.mark("p0_rs2")

            if stage >= 1:
                _run_phase(ctx, nc, [
                    ("gpsimd", gp, p0_gp),
                    ("tensor", pe, p0_pe),
                    ("vector", dv, p0_dv),
                    ("scalar", ac, p0_ac),
                ])

        with contextlib.ExitStack() as s1:
            SB1 = lambda n, s, t: s1.enter_context(nc.sbuf_tensor(n, s, t))
            PS1 = lambda n, s, t: s1.enter_context(nc.psum_tensor(n, s, t))
            wbuf = SB1("wbuf", [128, 2 * DM], BF16)
            sE1 = contextlib.ExitStack()
            SBE = lambda n, s, t: sE1.enter_context(nc.sbuf_tensor(n, s, t))
            convsrc = SBE("convsrc", [128, 2 * TW], BF16)
            convacc = SBE("convacc", [128, 2 * 1024], BF16)
            hidg = SBE("hidg", [128, 2 * 8 * 1024], BF16)
            hcg = SBE("hcg", [128, 8 * 1024], BF16)
            BgS = SBE("BgS", [128, 2 * 1024], BF16)
            CgS = SBE("CgS", [128, 2 * 1024], BF16)
            dtTf = SBE("dtTf", [128, 1024], F32)
            zf = SBE("zf", [128, 1024], F32)
            acsT = SBE("acsT", [128, 1024], F32)
            zrow = SBE("zrow", [128, 128], F32)
            Pg = SBE("Pg", [128, 2 * 1024], F32)
            hcd_t = SBE("hcd_t", [128, 2 * 128], BF16)
            btokS = SBE("btokS", [128, 2 * 128], BF16)
            dcT_t = SBE("dcT_t", [128, 128], BF16)
            eaT_t = SBE("eaT_t", [128, 128], F32)

            ps_in = [PS1(f"ps_in{i}", [128, 512], F32) for i in range(5)]
            BANKS_EVEN = [ps_in[0], ps_in[1], ps_in[2]]
            BANKS_ODD = [ps_in[3], ps_in[4], ps_in[2]]
            sCD = contextlib.ExitStack()
            ps_trf_full = sCD.enter_context(nc.psum_tensor("ps_trf", [128, 512], F32))
            ps_trf = ps_trf_full[:, 0:128]
            ps_trb_cd = sCD.enter_context(nc.psum_tensor("ps_trb_cd", [128, 1024], BF16))[:, 0:128]

            def pc_gp():
                gp.waitm("p0_hT")
                gp.waitm("p0_sumsq2")
                gp.dma(wbuf[:, 0:DM], win[FT_DT * 128:(FT_DT + 1) * 128, :])
                gp.mark("w_dt")

            def pc_pe():
                pe.waitm("w_dt")
                pe.waitm("p0_hT")
                for dt in range(NDT):
                    for gi, (c0, n) in enumerate(G_HBC):
                        last = (gi == 2 and dt == NDT - 1)
                        pe.op("matmul", BANKS_EVEN[gi][:, 0:n],
                              wbuf[:, dt * 128:(dt + 1) * 128],
                              hT[:, dt * TW + c0: dt * TW + c0 + n],
                              start=(dt == 0), stop=(dt == NDT - 1), inc=1 if last else 0)
                pe.mark("mm_dt")

            def pc_ac():
                ac.waitm("mm_dt")
                offsets = [0, 344, 688]
                for gi in (2, 0, 1):
                    c0, n = G_HBC[gi]
                    ac.op("activation", convsrc[:, offsets[gi]:offsets[gi] + n],
                          BANKS_EVEN[gi][:, 0:n], AF.Identity, bias=dtbias[:])
                    if gi == 2:
                        ac.mark("ev2_dt")
                ac.mark("ev_dt")

            if stage >= 2:
                _run_phase(ctx, nc, [
                    ("gpsimd", gp, pc_gp), ("tensor", pe, pc_pe), ("scalar", ac, pc_ac),
                ])

            def pd_dv():
                z = convsrc[:, 3:3 + 1024]
                dv.op("memset", zrow[:], 0.0)
                dv.waitm("ev_dt")
                dv.op("tensor_scalar_max", dtTf[:], z, 0.0)
                dv.mark("dt_relu")
                dv.waitm("dt_absln")
                dv.op("tensor_add", dtTf[:], dtTf[:], zf[:])
                dv.dmark("dt_done")
                dv.waitm("c_negA")
                dv.op("tensor_scalar_mul", zf[:], dtTf[:], negA[:])
                dv.mark("adt")
                for c in range(NCH):
                    dv.op("tensor_tensor_scan", acsT[:, c * 128:(c + 1) * 128],
                          zf[:, c * 128:(c + 1) * 128], zrow[:], 0.0, ALU.add, ALU.add)
                dv.dmark("acs")
                dv.op("memset", offs[:], 0.0)
                for c in range(NCH):
                    dv.op("tensor_add", offs[:, c + 1:c + 2], offs[:, c:c + 1],
                          acsT[:, c * 128 + 127:c * 128 + 128])
                dv.dmark("offs")
                for c in range(NCH):
                    dv.waitm(f"cp_dctr{c}")
                    dv.op("tensor_copy", dc_all[:, c * 128:(c + 1) * 128], ps_trb_cd)
                    dv.mark(f"cp_dc{c}")

            def pd_ac():
                z = convsrc[:, 3:3 + 1024]
                ac.waitm("dt_relu")
                ac.op("activation", zf[:], z, AF.Abs)
                ac.op("activation", zf[:], zf[:], AF.Exp, scale=-1.0)
                ac.op("activation", zf[:], zf[:], AF.Ln, bias=onecol[:])
                ac.mark("dt_absln")
                for c in range(NCH):
                    acslice = acsT[:, c * 128:(c + 1) * 128]
                    alast = acsT[:, c * 128 + 127:c * 128 + 128]
                    ac.waitm(f"cp_dtt{c}")
                    ac.op("copy", dt_tok[:, c * 128:(c + 1) * 128], ps_trf)
                    ac.mark(f"cp_dtev{c}")
                    ac.waitm(f"cp_acst{c}")
                    ac.op("copy", acs_tok[:, c * 128:(c + 1) * 128], ps_trf)
                    ac.mark(f"cp_ev{c}")
                    if c >= 1:
                        ac.waitm(f"cp_dctr{c-1}")
                    ac.op("activation", dcT_t[:], acslice, AF.Exp, scale=-1.0, bias=alast)
                    ac.mark(f"cp_dcT{c}")
                    if c >= 1:
                        ac.waitm(f"cp_eat{c-1}")
                    ac.op("activation", eaT_t[:], acslice, AF.Exp, scale=0.0, bias=alast)
                    ac.mark(f"cp_eaT{c}")
                    ac.waitm(f"cp_eat{c}")
                    ac.op("copy", expa[:, c * 128:(c + 1) * 128], ps_trf)
                    ac.mark(f"cp_expa{c}")

            def pd_pe():
                for c in range(NCH):
                    if c == 0:
                        pe.waitm("dt_done")
                        pe.waitm("acs")
                    else:
                        pe.waitm(f"cp_expa{c-1}")
                    pe.op("transpose", ps_trf, dtTf[:, c * 128:(c + 1) * 128], idf[:], inc=1)
                    pe.mark(f"cp_dtt{c}")
                    pe.waitm(f"cp_dtev{c}")
                    pe.op("transpose", ps_trf, acsT[:, c * 128:(c + 1) * 128], idf[:], inc=1)
                    pe.mark(f"cp_acst{c}")
                    pe.waitm(f"cp_dcT{c}")
                    if c >= 1:
                        pe.waitm(f"cp_dc{c-1}")
                    pe.op("transpose", ps_trb_cd, dcT_t[:], idbf[:], inc=1)
                    pe.mark(f"cp_dctr{c}")
                    pe.waitm(f"cp_eaT{c}")
                    pe.waitm(f"cp_ev{c}")
                    pe.op("transpose", ps_trf, eaT_t[:], idf[:], inc=1)
                    pe.mark(f"cp_eat{c}")

            def pd_gp2():
                gp.waitm(f"cp_dc{NCH-1}")
                gp.dma(sind[:, 1024:2048], dc_all[:])
                gp.waitm(f"cp_dtev{NCH-1}")
                gp.dma(sind[:, 3072:4096], dt_tok[:])

            def pd_sy():
                sy.waitm("acs")
                sy.dma(acsb, acsT[:])
                sy.waitm("dt_done")
                sy.dma(dtd, dtTf[:])
                sy.waitm("offs")
                sy.dma(offsd, offs[:])
                sy.mark("offsd_done")
                sy.waitm("offsd_done")
                sy.waitm(f"cp_expa{NCH-1}")
                sy.dma(sind[:, 0:1024], expa[:])
                sy.waitm(f"cp_ev{NCH-1}")
                sy.dma(sind[:, 2048:3072], acs_tok[:])
                sy.dma(ccin2, offsd[:, 8:9].rearrange("h one -> (one h)").unsqueeze(0),
                      allow_slow_non_contiguous=True)
                sy.mark("dt_spill")

            if stage >= 3:
                _run_phase(ctx, nc, [
                    ("vector", dv, pd_dv), ("scalar", ac, pd_ac),
                    ("tensor", pe, pd_pe), ("sync", sy, pd_sy),
                    ("gpsimd", gp, pd_gp2),
                ])

            sCD.close()
            sE2 = contextlib.ExitStack()
            ps_tr2 = [sE2.enter_context(nc.psum_tensor(f"ps_trb{i}", [128, 1024], BF16))[:, 0:128] for i in range(2)]
            ps_sc = sE2.enter_context(nc.psum_tensor("ps_sc", [128, 512], F32))[:, 0:128]

            p1a_tiles = []
            for g in range(GQ):
                p1a_tiles += [FT_HID0 + g * 8 + i for i in range(8)] + [FT_B0 + g, FT_C0 + g]

            def tile_meta(ft):
                if ft < FT_B0:
                    g, kind, fco = (ft - FT_HID0) // 8, "h", ft - FT_HID0
                elif ft < FT_C0:
                    g, kind, fco = ft - FT_B0, "b", 64 + (ft - FT_B0)
                else:
                    g, kind, fco = ft - FT_C0, "c", 72 + (ft - FT_C0)
                return g, kind, fco

            def pe_gp():
                for i, ft in enumerate(p1a_tiles):
                    slot = i % 2
                    if i == 0:
                        gp.waitm("mm_dt")
                    if i >= 2:
                        gp.waitm(f"mm{p1a_tiles[i-2]}")
                    gp.dma(wbuf[:, slot * DM:(slot + 1) * DM], win[ft * 128:(ft + 1) * 128, :])
                    gp.mark(f"w{ft}")

            def pe_pe():
                for i, ft in enumerate(p1a_tiles):
                    slot = i % 2
                    bank3 = BANKS_EVEN if (i % 2 == 1) else BANKS_ODD
                    pe.waitm(f"w{ft}")
                    if i == 0:
                        pe.waitm("ev2_dt")
                    elif i == 1:
                        pe.waitm("ev_dt")
                        pe.waitm(f"ev2{p1a_tiles[0]}")
                    if i >= 2:
                        pe.waitm(f"ev{p1a_tiles[i-2]}")
                        pe.waitm(f"ev2{p1a_tiles[i-1]}")
                    for dt in range(NDT):
                        for gi, (c0, n) in enumerate(G_HBC):
                            last = (gi == 2 and dt == NDT - 1)
                            pe.op("matmul", bank3[gi][:, 0:n],
                                  wbuf[:, slot * DM + dt * 128: slot * DM + (dt + 1) * 128],
                                  hT[:, dt * TW + c0: dt * TW + c0 + n],
                                  start=(dt == 0), stop=(dt == NDT - 1), inc=1 if last else 0)
                    pe.mark(f"mm{ft}")
                    if stage >= 5:
                        g, kind, _ = tile_meta(ft)
                        if kind == "c" and g >= 1:
                            states_pe(g - 1)
                if stage >= 5:
                    states_pe(GQ - 1)

            def states_pe(g):
                sched = []
                for c in range(NCH):
                    sched.append(("b", c, 0))
                    for fb in range(8):
                        sched.append(("h", c, fb))
                for i, (kind, c, fb) in enumerate(sched):
                    slot = (g * 72 + i) % 2
                    gi2 = g * 72 + i - 2
                    if gi2 >= 0:
                        g2, i2 = gi2 // 72, gi2 % 72
                        k2, c2, fb2 = ("b", i2 // 9, 0) if i2 % 9 == 0 else ("h", i2 // 9, i2 % 9 - 1)
                        pe.waitm(f"st_rdb{g2}_{c2}" if k2 == "b" else f"st_rdh{g2}_{c2}_{fb2}")
                    if kind == "b":
                        if c == 0:
                            pe.waitm(f"silu{FT_B0 + g}")
                        pe.op("transpose", ps_tr2[slot][:],
                              BgS[:, (g % 2) * 1024 + c * 128:(g % 2) * 1024 + (c + 1) * 128],
                              idbf[:], inc=1)
                        pe.mark(f"st_btr{g}_{c}")
                    else:
                        if c == 0 and fb < 8:
                            pe.waitm(f"silu{FT_HID0 + g * 8 + fb}")
                        pe.op("transpose", ps_tr2[slot][:],
                              hidg[:, (g % 2) * 8192 + fb * 1024 + c * 128:
                                    (g % 2) * 8192 + fb * 1024 + (c + 1) * 128],
                              idbf[:], inc=1)
                        pe.mark(f"st_htr{g}_{c}_{fb}")
                        pe.waitm(f"st_hcd{g}_{c}_{fb}")
                        prevsc = (g, c, fb - 1) if fb >= 1 else ((g, c - 1, 7) if c >= 1 else
                                                                ((g - 1, NCH - 1, 7) if g >= 1 else None))
                        if prevsc is not None:
                            pe.waitm("st_pupd%d_%d_%d" % prevsc)
                        pe.op("matmul", ps_sc[:],
                              btokS[:, (c % 2) * 128:(c % 2) * 128 + 128],
                              hcd_t[:, (fb % 2) * 128:(fb % 2) * 128 + 128],
                              start=True, stop=True, inc=1)
                        pe.mark(f"st_sc{g}_{c}_{fb}")

            def pe_ac():
                for i, ft in enumerate(p1a_tiles):
                    slot = i % 2
                    bank3 = BANKS_EVEN if (i % 2 == 1) else BANKS_ODD
                    cs = convsrc[:, slot * TW:(slot + 1) * TW]
                    g, kind, fco = tile_meta(ft)
                    ac.waitm(f"mm{ft}")
                    if i == 0:
                        ac.waitm("dt_relu")
                        ac.waitm("dt_absln")
                    if i >= 2:
                        ac.waitm(f"taps{p1a_tiles[i-2]}")
                    offsets = [0, 344, 688]
                    for gi in (2, 0, 1):
                        c0, n = G_HBC[gi]
                        ac.op("copy", cs[:, offsets[gi]:offsets[gi] + n], bank3[gi][:, 0:n])
                        if gi == 2:
                            ac.mark(f"ev2{ft}")
                    ac.mark(f"ev{ft}")
                    ac.waitm(f"taps{ft}")
                    if kind == "h":
                        fb = fco % 8
                        dest = hidg[:, (g % 2) * 8192 + fb * 1024:(g % 2) * 8192 + (fb + 1) * 1024]
                        if g >= 2 and fb == 0:
                            ac.waitm(f"hspill{g-2}")
                            if stage >= 5:
                                ac.waitm(f"st_htr{g-2}_{NCH-1}_7")
                    elif kind == "b":
                        dest = BgS[:, (g % 2) * 1024:(g % 2) * 1024 + 1024]
                        if g >= 2:
                            ac.waitm(f"bspill{g-2}")
                            if stage >= 5:
                                ac.waitm(f"st_btr{g-2}_{NCH-1}")
                    else:
                        dest = CgS[:, (g % 2) * 1024:(g % 2) * 1024 + 1024]
                        if g >= 2:
                            ac.waitm(f"cspill{g-2}")
                    ac.op("activation", dest, convacc[:, slot * 1024:(slot + 1) * 1024],
                          AF.Silu, bias=cbS[:, fco:fco + 1])
                    ac.dmark(f"silu{ft}")

            def pe_dv():
                for i, ft in enumerate(p1a_tiles):
                    slot = i % 2
                    cs = convsrc[:, slot * TW:(slot + 1) * TW]
                    ca = convacc[:, slot * 1024:(slot + 1) * 1024]
                    g, kind, fco = tile_meta(ft)
                    dv.waitm(f"ev{ft}")
                    if i >= 2:
                        dv.waitm(f"silu{p1a_tiles[i-2]}")
                    dv.op("tensor_scalar_mul", ca[:], cs[:, 0:1024], cw[:, fco * 4:fco * 4 + 1])
                    for tp_ in (1, 2, 3):
                        dv.op("scalar_tensor_tensor", ca[:], cs[:, tp_:tp_ + 1024],
                              cw[:, fco * 4 + tp_:fco * 4 + tp_ + 1], ca[:], ALU.mult, ALU.add)
                    dv.mark(f"taps{ft}")
                    if stage >= 5:
                        if kind == "c" and g >= 1:
                            states_dv(g - 1)
                if stage >= 5:
                    states_dv(GQ - 1)

            def states_dv(g):
                for c in range(NCH):
                    dv.waitm(f"cp_dc{c}")
                    dv.waitm(f"cp_dtev{c}")
                    dv.waitm(f"cp_expa{c}")
                    dv.waitm(f"st_btr{g}_{c}")
                    slot_b = (g * 72 + c * 9) % 2
                    if c >= 2:
                        dv.waitm(f"st_sc{g}_{c-2}_7")
                    elif g >= 1:
                        dv.waitm(f"st_sc{g-1}_{NCH-2+c}_7")
                    dv.op("tensor_copy", btokS[:, (c % 2) * 128:(c % 2) * 128 + 128],
                          ps_tr2[slot_b][:])
                    dv.mark(f"st_rdb{g}_{c}")
                    for fb in range(8):
                        i = c * 9 + 1 + fb
                        slot = (g * 72 + i) % 2
                        dv.waitm(f"st_htr{g}_{c}_{fb}")
                        if c == 0 and fb == 0 and g >= 1:
                            dv.waitm(f"hcspill{g-1}")
                        dtr = dt_tok[:, c * 128 + g * 16 + 2 * fb: c * 128 + g * 16 + 2 * fb + 2]
                        dtr = dtr.unsqueeze(2).broadcast_to([128, 2, 64])
                        hcdst = hcg[:, c * 1024 + fb * 128: c * 1024 + (fb + 1) * 128]
                        dv.op("tensor_tensor", hcdst.rearrange("p (a b) -> p a b", a=2),
                              ps_tr2[slot][:].rearrange("p (a b) -> p a b", a=2), dtr, ALU.mult)
                        dv.mark(f"st_rdh{g}_{c}_{fb}")
                        prev2 = (g, c, fb - 2) if fb >= 2 else ((g, c - 1, 6 + fb) if c >= 1 else
                                                               ((g - 1, NCH - 1, 6 + fb) if g >= 1 else None))
                        if prev2 is not None:
                            dv.waitm("st_sc%d_%d_%d" % prev2)
                        dcr = dc_all[:, c * 128 + g * 16 + 2 * fb: c * 128 + g * 16 + 2 * fb + 2]
                        dcr = dcr.unsqueeze(2).broadcast_to([128, 2, 64])
                        dv.op("tensor_tensor",
                              hcd_t[:, (fb % 2) * 128:(fb % 2) * 128 + 128].rearrange("p (a b) -> p a b", a=2),
                              hcdst.rearrange("p (a b) -> p a b", a=2), dcr, ALU.mult)
                        dv.mark(f"st_hcd{g}_{c}_{fb}")
                        dv.waitm(f"st_sc{g}_{c}_{fb}")
                        if c == 0:
                            if fb == 0:
                                if g >= 1:
                                    dv.waitm(f"psnap{g-1}_7")
                        elif c == 1 and fb == 0 and g >= 1:
                            dv.waitm(f"ccin{g-1}")
                        if c >= 2 and fb == 0:
                            dv.waitm(f"psnap{g}_{c-1}")
                        for hh in range(2):
                            hcol = (2 * fb + hh) * 64
                            cur = Pg[:, (c % 2) * 1024 + hcol:(c % 2) * 1024 + hcol + 64]
                            nxt = Pg[:, ((c + 1) % 2) * 1024 + hcol:((c + 1) % 2) * 1024 + hcol + 64]
                            if c == 0:
                                dv.op("tensor_copy", nxt, ps_sc[:, hh * 64:hh * 64 + 64])
                            else:
                                dv.op("scalar_tensor_tensor", nxt, cur,
                                      expa[:, c * 128 + g * 16 + 2 * fb + hh:
                                           c * 128 + g * 16 + 2 * fb + hh + 1],
                                      ps_sc[:, hh * 64:hh * 64 + 64], ALU.mult, ALU.add)
                        if fb == 7:
                            dv.dmark(f"st_pupd{g}_{c}_{fb}")
                        else:
                            dv.mark(f"st_pupd{g}_{c}_{fb}")
                dv.dmark(f"st_vdone{g}")

            def pe_sy():
                for g in range(GQ):
                    for fb in range(8):
                        ft = FT_HID0 + g * 8 + fb
                        sy.waitm(f"silu{ft}")
                        sy.dma(hidb[(g * 8 + fb) * 128:(g * 8 + fb + 1) * 128, :],
                               hidg[:, (g % 2) * 8192 + fb * 1024:(g % 2) * 8192 + (fb + 1) * 1024])
                    sy.mark(f"hspill{g}")
                    sy.waitm(f"silu{FT_B0 + g}")
                    sy.dma(bcb[(2 * g) * 128:(2 * g + 1) * 128, :],
                           BgS[:, (g % 2) * 1024:(g % 2) * 1024 + 1024])
                    sy.mark(f"bspill{g}")
                    sy.waitm(f"silu{FT_C0 + g}")
                    sy.dma(bcb[(2 * g + 1) * 128:(2 * g + 2) * 128, :],
                           CgS[:, (g % 2) * 1024:(g % 2) * 1024 + 1024])
                    sy.mark(f"cspill{g}")
                    if stage >= 5:
                        for c in range(1, NCH):
                            sy.waitm(f"st_pupd{g}_{c-1}_7")
                            sy.dma(pclb[(g * 8 + c) * 128:(g * 8 + c + 1) * 128, :],
                                   Pg[:, (c % 2) * 1024:(c % 2) * 1024 + 1024])
                            sy.mark(f"psnap{g}_{c}")
                        sy.waitm(f"st_pupd{g}_{NCH-1}_7")
                        sy.dma(ccin[:, g * 1024:(g + 1) * 1024], Pg[:, 0:1024])
                        sy.mark(f"ccin{g}")
                        sy.waitm(f"st_vdone{g}")
                        sy.dma(hcb[g * 128:(g + 1) * 128, :], hcg[:])
                        sy.mark(f"hcspill{g}")
                        sy.mark(f"st_done{g}")

            if stage >= 4:
                _run_phase(ctx, nc, [
                    ("gpsimd", gp, pe_gp),
                    ("tensor", pe, pe_pe),
                    ("scalar", ac, pe_ac),
                    ("vector", dv, pe_dv),
                    ("sync", sy, pe_sy),
                ])

            sE1.close()
            sE2.close()

            if stage >= 6:
                with contextlib.ExitStack() as sF:
                    SBF = lambda n, s, t: sF.enter_context(nc.sbuf_tensor(n, s, t))
                    S_acc = SBF("S_acc", [128, 8192], F32)
                    gbuf = SBF("gbuf", [128, 2 * 1024], BF16)
                    gmt_sb = SBF("gmt_sb", [8, 128], F32)
                    mcol_sb = SBF("mcol_sb", [8, 1], F32)
                    atot_sb = SBF("atot_sb", [8, 128], F32)
                    wsb = SBF("wsb", [8, 128], F32)
                    wTsb = SBF("wTsb", [128, 8], F32)
                    wjb = SBF("wjb", [128, 128], F32)
                    wjT = SBF("wjT", [128, 128], F32)
                    sjl = SBF("sjl", [128, 2 * 4096], F32)
                    ps_g1 = sF.enter_context(nc.psum_tensor("ps_g1", [128, 512], F32))
                    ps_g2 = sF.enter_context(nc.psum_tensor("ps_g2", [128, 512], F32))

                    def pf_gp():
                        if not NO_SIN:
                            gp.waitm("dt_spill")
                            for g in range(GQ):
                                gp.waitm(f"ccin{g}")
                            gp.wait_queue_idle(gp)
                            gp.op("collective_compute", "AllGather", ALU.bypass,
                                  replica_groups=RG8, ins=[ccin.opt()], outs=[ccout.opt()],
                                  inc=1, semname="CK")
                            gp.op("collective_compute", "AllGather", ALU.bypass,
                                  replica_groups=RG8, ins=[ccin2.opt()], outs=[ccout2.opt()],
                                  inc=1, semname="CK")
                            gp.wait_count("CK", 2)
                        for j in range(64):
                            slot = j % 2
                            if j == 0:
                                gp.waitm(f"mm{p1a_tiles[78]}")
                            elif j == 1:
                                gp.waitm(f"mm{p1a_tiles[79]}")
                            else:
                                gp.waitm(f"gmm{j-2}")
                            gp.dma(wbuf[:, slot * DM:(slot + 1) * DM], win[j * 128:(j + 1) * 128, :])
                            gp.mark(f"gw{j}")

                    def pf_pe():
                        for j in range(64):
                            slot = j % 2
                            banks = [ps_in[0], ps_in[1]] if j % 2 == 0 else [ps_in[3], ps_in[4]]
                            pe.waitm(f"gw{j}")
                            if j == 0:
                                pe.waitm(f"ev{p1a_tiles[79]}")
                            elif j == 1:
                                pe.waitm(f"ev{p1a_tiles[78]}")
                            else:
                                pe.waitm(f"gev{j-2}")
                            for dt in range(NDT):
                                for gi, (c0, n) in enumerate(G_GATE):
                                    pe.op("matmul", banks[gi][:, 0:n],
                                          wbuf[:, slot * DM + dt * 128: slot * DM + (dt + 1) * 128],
                                          hT[:, dt * TW + c0: dt * TW + c0 + n],
                                          start=(dt == 0), stop=(dt == NDT - 1),
                                          inc=1 if gi == 1 else 0)
                            pe.mark(f"gmm{j}")
                        if NO_SIN:
                            return
                        pe.waitm("f_w")
                        pe.op("matmul", ps_g1[:, 0:128], gmt_sb[:], atot_sb[:],
                              start=True, stop=True, inc=1)
                        pe.mark("f_cmm")
                        pe.waitm("f_wm")
                        pe.op("transpose", ps_g2[:, 0:8], wsb[:], idf[0:8, 0:8], inc=1)
                        pe.mark("f_wT")
                        for j in range(8):
                            pe.waitm(f"f_wjb{j}")
                            if j >= 1:
                                pe.waitm(f"f_wjev{j-1}")
                            pe.op("transpose", ps_g2[:, 128:256], wjT[:], idf[:], inc=1)
                            pe.mark(f"f_wjt{j}")

                    def pf_ac():
                        ac.waitm(f"st_vdone{GQ-1}")
                        for j in range(64):
                            slot = j % 2
                            banks = [ps_in[0], ps_in[1]] if j % 2 == 0 else [ps_in[3], ps_in[4]]
                            ac.waitm(f"gmm{j}")
                            if j >= 2:
                                ac.waitm(f"gspill{j-2}")
                            for gi in range(2):
                                ac.op("copy", gbuf[:, slot * 1024 + gi * 512: slot * 1024 + (gi + 1) * 512],
                                      banks[gi][:, 0:512])
                            ac.dmark(f"gev{j}")
                        if NO_SIN:
                            return
                        ac.waitm("f_cmm")
                        ac.op("activation", wsb[:], ps_g1[0:8, 0:128], AF.Exp)
                        ac.mark("f_exp")
                        ac.waitm("f_wT")
                        ac.op("copy", wTsb[:], ps_g2[:, 0:8])
                        ac.mark("f_wTsb")
                        for j in range(8):
                            ac.waitm("f_wTsb")
                            if j >= 1:
                                ac.waitm(f"f_wjt{j-1}")
                            ac.op("activation", wjT[:], acs_tok[:, 0:128],
                                  AF.Identity, scale=0.0, bias=wTsb[:, j:j + 1])
                            ac.mark(f"f_wjb{j}")

                    def pf_dv():
                        if NO_SIN:
                            return
                        dv.waitm("f_exp")
                        dv.op("tensor_scalar_mul", wsb[:], wsb[:], mcol_sb[:])
                        dv.mark("f_wm")
                        for j in range(8):
                            dv.waitm(f"f_wjt{j}")
                            dv.op("tensor_copy", wjb[:], ps_g2[:, 128:256])
                            dv.mark(f"f_wjev{j}")
                            for half in range(2):
                                dv.waitm(f"f_sl{j}_{half}")
                                wrep = wjb[:, half * 64:(half + 1) * 64].unsqueeze(2).broadcast_to([128, 64, 64])
                                s3 = sjl[:, (half % 2) * 4096:(half % 2) * 4096 + 4096].rearrange(
                                    "n (h p) -> n h p", h=64)
                                d3 = S_acc[:, half * 4096:(half + 1) * 4096].rearrange(
                                    "n (h p) -> n h p", h=64)
                                if j == 0:
                                    dv.op("tensor_tensor", d3, s3, wrep, ALU.mult)
                                else:
                                    dv.op("tensor_tensor", s3, s3, wrep, ALU.mult)
                                    dv.op("tensor_add", d3, d3, s3)
                                dv.mark(f"f_acc{j}_{half}")
                        dv.waitm("f_acc7_1")
                        dv.op("tensor_copy", S_in_bf[:], S_acc[:])
                        dv.dmark("f_sin")

                    def pf_sy():
                        sy.waitm(f"st_vdone{GQ-1}")
                        sy.waitm(f"st_done{GQ-1}")
                        sy.dma(gmt_sb[:], gmt)
                        sy.dma(mcol_sb[:], mcolA)
                        sy.mark("f_consts")
                        if NO_SIN:
                            for j in range(64):
                                sy.waitm(f"gev{j}")
                                sy.dma(gateb[j * 128:(j + 1) * 128, :],
                                       gbuf[:, (j % 2) * 1024:(j % 2) * 1024 + 1024])
                                sy.mark(f"gspill{j}")
                            return
                        for j in range(64):
                            sy.waitm(f"gev{j}")
                            sy.dma(gateb[j * 128:(j + 1) * 128, :],
                                   gbuf[:, (j % 2) * 1024:(j % 2) * 1024 + 1024])
                            sy.mark(f"gspill{j}")
                        sy.wait_count("CK", 2)
                        sy.dma(atot_sb[:], ccout2)
                        sy.mark("f_w")
                        for j in range(8):
                            for half in range(2):
                                if j * 2 + half >= 2:
                                    pj, ph = (j * 2 + half - 2) // 2, (j * 2 + half - 2) % 2
                                    sy.waitm(f"f_acc{pj}_{ph}")
                                sy.dma(sjl[:, (half % 2) * 4096:(half % 2) * 4096 + 4096],
                                       ccout[j * 128:(j + 1) * 128, half * 4096:(half + 1) * 4096])
                                sy.mark(f"f_sl{j}_{half}")
                        sy.waitm("f_sin")
                        sy.dma(sind[:, 4096:8192], S_acc[:, 0:4096])

                    _run_phase(ctx, nc, [
                        ("sync", sy, pf_sy),
                        ("gpsimd", gp, pf_gp),
                        ("tensor", pe, pf_pe),
                        ("scalar", ac, pf_ac),
                        ("vector", dv, pf_dv),
                    ])

            if stage >= 6:
                sDT2.close()
                sH.close()
        if stage >= 7:
            with contextlib.ExitStack() as sG:
                SBG = lambda n, s, t: sG.enter_context(nc.sbuf_tensor(n, s, t))
                PSG = lambda n, s, t: sG.enter_context(nc.psum_tensor(n, s, t))
                hid2 = SBG("hid2", [128, 2 * 8192], BF16)
                hc2 = SBG("hc2", [128, 2 * 8192], BF16)
                gate2 = SBG("gate2", [128, 2 * 8192], BF16)
                Bg2 = SBG("Bg2", [128, 2 * 1024], BF16)
                Cg2 = SBG("Cg2", [128, 2 * 1024], BF16)
                pl2 = SBG("pl2", [128, 2 * 1024], BF16)
                csfl = SBG("csfl", [1, 2 * 2048], F32)
                eo_all = SBG("eo_all", [128, 8 * 128], F32)
                eoT_t = SBG("eoT_t", [128, 128], F32)
                LtB = SBG("LtB", [128, 2 * 1024], BF16)
                EAB = SBG("EAB", [128, 2 * 1024], BF16)
                MthB = SBG("MthB", [128, 2 * 1024], BF16)
                ChsB = SBG("ChsB", [128, 2 * 1024], BF16)
                t1f = SBG("t1f", [128, 1024], F32)
                GtB = SBG("GtB", [128, 128], BF16)
                Pfull = SBG("Pfull", [128, 2 * 1024], BF16)
                y2 = SBG("y2", [128, 8192], BF16)
                sq2 = SBG("sq2", [128, 1024], BF16)
                rstd_bf = SBG("rstd_bf", [128, 1024], BF16)

                psR = PSG("psR", [128, 1024], F32)
                psG_b = PSG("psG_b", [128, 512], F32)
                psY = [PSG(f"psY{i}", [128, 512], F32)[:, 0:128] for i in range(2)]
                psS = PSG("psS", [128, 1024], F32)

                def head_slices(c, g, fb):
                    half = fb // 4
                    lh0 = 2 * fb - half * 8
                    return half, lh0

                def pg_sy():
                    sy.waitm("f_sin")
                    sy.waitm("gspill63")
                    for g in range(G_LIM[0] if G_LIM else GQ):
                        slot = g % 2
                        if g >= 2:
                            sy.waitm(f"g_pe_done{g-2}")
                            sy.waitm(f"g_dv_done{g-2}")
                        for fb in range(8):
                            sy.dma(hid2[:, slot * 8192 + fb * 1024: slot * 8192 + (fb + 1) * 1024],
                                   hidb[(g * 8 + fb) * 128:(g * 8 + fb + 1) * 128, :])
                        sy.mark(f"rl_hid{g}")
                        sy.dma(hc2[:, slot * 8192:(slot + 1) * 8192], hcb[g * 128:(g + 1) * 128, :])
                        sy.mark(f"rl_hc{g}")
                        for fb in range(8):
                            sy.dma(gate2[:, slot * 8192 + fb * 1024: slot * 8192 + (fb + 1) * 1024],
                                   gateb[(g * 8 + fb) * 128:(g * 8 + fb + 1) * 128, :])
                        sy.mark(f"rl_gate{g}")
                        sy.dma(Bg2[:, slot * 1024:(slot + 1) * 1024], bcb[2 * g * 128:(2 * g + 1) * 128, :])
                        sy.dma(Cg2[:, slot * 1024:(slot + 1) * 1024], bcb[(2 * g + 1) * 128:(2 * g + 2) * 128, :])
                        sy.mark(f"rl_bc{g}")
                        for c in range(G_LIM[1] if G_LIM else NCH):
                            if g * 8 + c >= 2:
                                pc_, cc_ = (g * 8 + c - 2) // 8, (g * 8 + c - 2) % 8
                                sy.waitm(f"rowb{pc_}_{cc_}_1")
                            sy.dma(csfl[0:1, (c % 2) * 2048:(c % 2) * 2048 + 2048].rearrange(
                                       "one (h i) -> one h i", h=16),
                                   acsb[16 * g:16 * (g + 1), 128 * c:128 * (c + 1)].unsqueeze(0))
                            sy.mark(f"csfl{g}_{c}")
                        sy.waitm(f"g_dv_done{g}")
                        for fb in range(8):
                            sy.dma(ybuf[(g * 8 + fb) * 128:(g * 8 + fb + 1) * 128, :],
                                   y2[:, fb * 1024:(fb + 1) * 1024])
                        sy.mark(f"yspill{g}")

                def pg_gp():
                    gp.waitm("f_sin")
                    gp.waitm("gspill63")
                    for g in range(G_LIM[0] if G_LIM else GQ):
                        for c in range(1, G_LIM[1] if G_LIM else NCH):
                            if c >= 3:
                                gp.waitm(f"pf{g}_{c-2}")
                            elif g >= 1:
                                gp.waitm(f"pf{g-1}_{7 if c == 1 else 6}")
                            gp.dma(pl2[:, (c % 2) * 1024:(c % 2) * 1024 + 1024],
                                   pclb[(g * 8 + c) * 128:(g * 8 + c + 1) * 128, :])
                            gp.mark(f"plrl{g}_{c}")

                def pg_ac():
                    ac.waitm("f_sin")
                    for c in range(NCH):
                        ac.waitm("offs")
                        if c >= 1:
                            ac.waitm(f"eot{c-1}")
                        ac.op("activation", eoT_t[:], acs_tok[:, 0:128], AF.Exp,
                              scale=0.0, bias=offs[:, c:c + 1])
                        ac.mark(f"eoT{c}")
                        ac.waitm(f"eot{c}")
                        ac.op("copy", eo_all[:, c * 128:(c + 1) * 128], psG_b[:, 0:128])
                        ac.mark(f"eo{c}")
                    for g in range(G_LIM[0] if G_LIM else GQ):
                        slot = g % 2
                        for c in range(G_LIM[1] if G_LIM else NCH):
                            for half in range(2):
                                hp = ((g * 8 + c) * 2 + half) % 2
                                ac.waitm(f"t2_{g}_{c}_{half}")
                                ph = (g * 8 + c) * 2 + half - 2
                                if ph >= 0:
                                    pg_, pc_, phf = ph // 16, (ph // 2) % 8, ph % 2
                                    ac.waitm(f"mth{pg_}_{pc_}_{phf}")
                                    ac.waitm(f"chs{pg_}_{pc_}_{phf}")
                                ac.op("activation", LtB[:, hp * 1024:(hp + 1) * 1024],
                                      t1f[:], AF.Exp)
                                ac.mark(f"lt{g}_{c}_{half}")
                                ac.waitm(f"rowb{g}_{c}_{half}")
                                ac.op("activation", EAB[:, hp * 1024:(hp + 1) * 1024],
                                      psR[:], AF.Exp)
                                ac.mark(f"ea{g}_{c}_{half}")
                        for fb in range(8):
                            ac.waitm(f"rl_gate{g}")
                            if g >= 1:
                                ac.waitm(f"g_dv_done{g-1}")
                            ac.op("activation", gate2[:, slot * 8192 + fb * 1024: slot * 8192 + (fb + 1) * 1024],
                                  gate2[:, slot * 8192 + fb * 1024: slot * 8192 + (fb + 1) * 1024],
                                  AF.Silu)
                            ac.mark(f"gsl{g}_{fb}")
                        if not NO_GNORM:
                            for fb in range(8):
                                ac.waitm(f"yg{g}_{fb}")
                                if fb >= 1:
                                    ac.waitm(f"sqmm{g}_{fb-1}")
                                ac.op("activation", sq2[:], y2[:, fb * 1024:(fb + 1) * 1024], AF.Square)
                                ac.mark(f"sq{g}_{fb}")
                            ac.waitm(f"ssq{g}")
                            ac.op("activation", rstd_bf[:], psS[:], AF.Ln, scale=1.0 / 1024.0, bias=epscol[:])
                            ac.op("activation", rstd_bf[:], rstd_bf[:], AF.Exp, scale=-0.5)
                            ac.mark(f"rstd{g}")

                def pg_pe():
                    pe.waitm("f_sin")
                    for c in range(NCH):
                        pe.waitm(f"eoT{c}")
                        if c >= 1:
                            pe.waitm(f"eo{c-1}")
                        pe.op("transpose", psG_b[:, 0:128], eoT_t[:], idf[:], inc=1)
                        pe.mark(f"eot{c}")
                    for g in range(G_LIM[0] if G_LIM else GQ):
                        slot = g % 2
                        for c in range(G_LIM[1] if G_LIM else NCH):
                            pe.waitm(f"rl_bc{g}")
                            if g == 0 and c == 0:
                                pe.waitm(f"eo{NCH-1}")
                            if g * 8 + c >= 1:
                                pidx = g * 8 + c - 1
                                pe.waitm(f"gtev{pidx // 8}_{pidx % 8}")
                            pe.op("matmul", psG_b[:, 0:128],
                                  Bg2[:, slot * 1024 + c * 128: slot * 1024 + (c + 1) * 128],
                                  Cg2[:, slot * 1024 + c * 128: slot * 1024 + (c + 1) * 128],
                                  start=True, stop=True, inc=1)
                            pe.mark(f"gmmG{g}_{c}")
                            for half in range(2):
                                pe.waitm(f"csfl{g}_{c}")
                                if (g * 8 + c, half) != (0, 0):
                                    ph = (g * 8 + c) * 2 + half - 1
                                    pg_, pc_, phf = ph // 16, (ph // 2) % 8, ph % 2
                                    pe.waitm(f"t1_{pg_}_{pc_}_{phf}")
                                    pe.waitm(f"ea{pg_}_{pc_}_{phf}")
                                for piece in range(2):
                                    pe.op("matmul", psR[:, piece * 512:(piece + 1) * 512], ones_fr[:],
                                          csfl[0:1, (c % 2) * 2048 + half * 1024 + piece * 512:
                                               (c % 2) * 2048 + half * 1024 + (piece + 1) * 512],
                                          start=True, stop=True, inc=(1 if piece == 1 else 0))
                                pe.mark(f"rowb{g}_{c}_{half}")
                                hp = ((g * 8 + c) * 2 + half) % 2
                                cp = (g * 8 + c) % 2
                                for fbl in range(4):
                                    fb = half * 4 + fbl
                                    bank = psY[fb % 2]
                                    _, lh0 = head_slices(c, g, fb)
                                    pe.waitm(f"mth{g}_{c}_{half}")
                                    pe.waitm(f"chs{g}_{c}_{half}")
                                    pe.waitm(f"pf{g}_{c}")
                                    pe.waitm(f"rl_hc{g}")
                                    yidx = (g * 8 + c) * 8 + fb - 2
                                    if yidx >= 0:
                                        yg_, yc_, yf_ = yidx // 64, (yidx // 8) % 8, yidx % 8
                                        pe.waitm(f"ya{yg_}_{yc_}_{yf_}")
                                    for hh in range(2):
                                        lh = lh0 + hh
                                        pe.op("matmul", bank[hh * 64:(hh + 1) * 64, :],
                                              hc2[:, slot * 8192 + c * 1024 + fb * 128 + hh * 64:
                                                  slot * 8192 + c * 1024 + fb * 128 + (hh + 1) * 64],
                                              MthB[:, hp * 1024 + lh * 128: hp * 1024 + (lh + 1) * 128],
                                              start=True, stop=False,
                                              tile_position=(0, 64 * hh), inc=0)
                                        pe.op("matmul", bank[hh * 64:(hh + 1) * 64, :],
                                              Pfull[:, cp * 1024 + (2 * fb + hh) * 64:
                                                    cp * 1024 + (2 * fb + hh + 1) * 64],
                                              ChsB[:, hp * 1024 + lh * 128: hp * 1024 + (lh + 1) * 128],
                                              start=False, stop=True,
                                              tile_position=(0, 64 * hh),
                                              inc=(1 if hh == 1 else 0))
                                    pe.mark(f"ymm{g}_{c}_{fb}")
                        if not NO_GNORM:
                            for fb in range(8):
                                pe.waitm(f"sq{g}_{fb}")
                                if fb == 0:
                                    pe.waitm(f"rstd{g-1}" if g >= 1 else "c_ones")
                                for piece in range(2):
                                    pe.op("matmul", psS[:, piece * 512:(piece + 1) * 512], ones_bb[:],
                                          sq2[:, piece * 512:(piece + 1) * 512],
                                          start=(fb == 0), stop=(fb == 7),
                                          inc=(1 if piece == 1 else 0))
                                pe.mark(f"sqmm{g}_{fb}")
                            pe.mark(f"ssq{g}")
                        pe.mark(f"g_pe_done{g}")

                def pg_dv():
                    if G_LIM:
                        dv.op("memset", y2[:], 0.0)
                    dv.mark("g_zero")
                    for g in range(G_LIM[0] if G_LIM else GQ):
                        slot = g % 2
                        for c in range(G_LIM[1] if G_LIM else NCH):
                            srep = S_in_bf[:, g * 1024:(g + 1) * 1024].rearrange("n (h p) -> n h p", h=16)
                            eorep = eo_all[:, c * 128 + g * 16: c * 128 + (g + 1) * 16]
                            eorep = eorep.unsqueeze(2).broadcast_to([128, 16, 64])
                            dv.waitm(f"eo{c}")
                            dv.waitm("f_sin")
                            cp = (g * 8 + c) % 2
                            pidx = g * 8 + c - 2
                            if pidx >= 0:
                                dv.waitm(f"pf_used{pidx // 8}_{pidx % 8}")
                            Pfs = Pfull[:, cp * 1024:(cp + 1) * 1024]
                            dv.op("tensor_tensor", Pfs.rearrange("n (h p) -> n h p", h=16),
                                  srep, eorep, ALU.mult)
                            if c != 0:
                                dv.waitm(f"plrl{g}_{c}")
                                dv.op("tensor_add", Pfs, Pfs, pl2[:, (c % 2) * 1024:(c % 2) * 1024 + 1024])
                            dv.mark(f"pf{g}_{c}")
                            dv.waitm(f"gmmG{g}_{c}")
                            dv.op("tensor_tensor", GtB[:], psG_b[:, 0:128], trimask[:], ALU.mult)
                            dv.mark(f"gtev{g}_{c}")
                            for half in range(2):
                                dv.waitm(f"rowb{g}_{c}_{half}")
                                if (g * 8 + c, half) != (0, 0):
                                    ph = (g * 8 + c) * 2 + half - 1
                                    pg_, pc_, phf = ph // 16, (ph // 2) % 8, ph % 2
                                    dv.waitm(f"lt{pg_}_{pc_}_{phf}")
                                colb = acs_tok[:, c * 128 + g * 16 + half * 8: c * 128 + g * 16 + half * 8 + 8]
                                colb = colb.unsqueeze(2).broadcast_to([128, 8, 128])
                                dv.op("tensor_tensor", t1f[:].rearrange("p (h i) -> p h i", h=8),
                                      psR[:].rearrange("p (h i) -> p h i", h=8), colb, ALU.subtract)
                                dv.mark(f"t1_{g}_{c}_{half}")
                                dv.op("tensor_scalar_min", t1f[:], t1f[:], 0.0)
                                dv.mark(f"t2_{g}_{c}_{half}")
                                hp = ((g * 8 + c) * 2 + half) % 2
                                grep = GtB[:].unsqueeze(1).broadcast_to([128, 8, 128])
                                dv.waitm(f"lt{g}_{c}_{half}")
                                ph = (g * 8 + c) * 2 + half - 2
                                if ph >= 0:
                                    pg_, pc_, phf = ph // 16, (ph // 2) % 8, ph % 2
                                    dv.waitm(f"ymm{pg_}_{pc_}_{phf * 4 + 3}")
                                dv.op("tensor_tensor",
                                      MthB[:, hp * 1024:(hp + 1) * 1024].rearrange("p (h i) -> p h i", h=8),
                                      LtB[:, hp * 1024:(hp + 1) * 1024].rearrange("p (h i) -> p h i", h=8),
                                      grep, ALU.mult)
                                dv.mark(f"mth{g}_{c}_{half}")
                                crep = Cg2[:, slot * 1024 + c * 128: slot * 1024 + (c + 1) * 128]
                                crep = crep.unsqueeze(1).broadcast_to([128, 8, 128])
                                dv.waitm(f"ea{g}_{c}_{half}")
                                dv.op("tensor_tensor",
                                      ChsB[:, hp * 1024:(hp + 1) * 1024].rearrange("p (h i) -> p h i", h=8),
                                      EAB[:, hp * 1024:(hp + 1) * 1024].rearrange("p (h i) -> p h i", h=8),
                                      crep, ALU.mult)
                                dv.mark(f"chs{g}_{c}_{half}")
                                for fbl in range(4):
                                    fb = half * 4 + fbl
                                    dv.waitm(f"ymm{g}_{c}_{fb}")
                                    dv.waitm(f"rl_hid{g}")
                                    if g >= 1 and c == 0 and fb == 0:
                                        dv.waitm(f"yspill{g-1}")
                                    dv.op("scalar_tensor_tensor",
                                          y2[:, fb * 1024 + c * 128: fb * 1024 + (c + 1) * 128],
                                          hid2[:, slot * 8192 + fb * 1024 + c * 128:
                                               slot * 8192 + fb * 1024 + (c + 1) * 128],
                                          dpcol[:, g * 8 + fb:g * 8 + fb + 1],
                                          psY[fb % 2], ALU.mult, ALU.add)
                                    dv.mark(f"ya{g}_{c}_{fb}")
                            dv.waitm(f"ymm{g}_{c}_7")
                            dv.mark(f"pf_used{g}_{c}")
                        for fb in range(8):
                            dv.waitm(f"gsl{g}_{fb}")
                            if not NO_GNORM:
                                dv.op("tensor_mul", y2[:, fb * 1024:(fb + 1) * 1024],
                                      y2[:, fb * 1024:(fb + 1) * 1024],
                                      gate2[:, slot * 8192 + fb * 1024: slot * 8192 + (fb + 1) * 1024])
                            else:
                                dv.op("nop", inc=1)
                            dv.mark(f"yg{g}_{fb}")
                        for fb in range(8):
                            if not NO_GNORM:
                                dv.waitm(f"rstd{g}")
                                dv.op("scalar_tensor_tensor", y2[:, fb * 1024:(fb + 1) * 1024],
                                      y2[:, fb * 1024:(fb + 1) * 1024],
                                      gncol[:, g * 8 + fb:g * 8 + fb + 1],
                                      rstd_bf[:], ALU.mult, ALU.mult)
                            else:
                                dv.op("nop", inc=1)
                        dv.op("drain", inc=1)
                        dv.mark(f"g_dv_done{g}")

                _run_phase(ctx, nc, [
                    ("sync", sy, pg_sy),
                    ("gpsimd", gp, pg_gp),
                    ("scalar", ac, pg_ac),
                    ("tensor", pe, pg_pe),
                    ("vector", dv, pg_dv),
                ])
        sSin.close()

        sDT.close()
        if stage >= 8:
            with contextlib.ExitStack() as sP2:
                SB2 = lambda n, s, t: sP2.enter_context(nc.sbuf_tensor(n, s, t))
                ySB = SB2("ySB", [128, 64 * 1024], BF16)
                wo2 = SB2("wo2", [128, 2 * 8192], BF16)
                xt2 = SB2("xt2", [128, 2 * 1024], F32)
                xaev = SB2("xaev", [128, 2 * 1024], F32)
                onepb = SB2("onepb", [128, 1024], F32)
                mrb = SB2("mrb", [128, 1024], F32)
                r1p = SB2("r1p", [1, TW], F32)
                r2p = SB2("r2p", [1, TW], F32)
                psO = [sP2.enter_context(nc.psum_tensor(f"psO{i}", [128, 1024], F32)) for i in range(2)]
                psB2 = sP2.enter_context(nc.psum_tensor("psB2", [128, 1024], F32))

                def ph_sy():
                    sy.waitm(f"yspill{GQ-1}")
                    sy.waitm(f"g_pe_done{GQ-1}")
                    sy.waitm(f"g_dv_done{GQ-1}")
                    for ft in range(64):
                        sy.dma(ySB[:, ft * 1024:(ft + 1) * 1024],
                               ybuf[ft * 128:(ft + 1) * 128, :])
                    sy.mark("rl_y")
                    for dt in range(NDT):
                        if dt >= 2:
                            sy.waitm(f"xaev{dt-2}")
                        sy.dma(xt2[:, (dt % 2) * 1024:(dt % 2) * 1024 + 1024],
                               xt[:, dt * TW + 3: dt * TW + 3 + 1024])
                        sy.mark(f"xrl{dt}")
                        sy.waitm(f"xadone{dt}")
                        sy.dma(xab[dt * 128:(dt + 1) * 128, :],
                               xaev[:, (dt % 2) * 1024:(dt % 2) * 1024 + 1024])
                        sy.mark(f"xaev{dt}")

                def ph_gp():
                    gp.waitm(f"yspill{GQ-1}")
                    gp.waitm(f"g_pe_done{GQ-1}")
                    gp.waitm(f"g_dv_done{GQ-1}")
                    for dt in range(NDT):
                        if dt >= 2:
                            gp.waitm(f"omm{dt-2}")
                        gp.dma(wo2[:, (dt % 2) * 8192:(dt % 2) * 8192 + 8192],
                               wout[dt * 128:(dt + 1) * 128, :])
                        gp.mark(f"wo{dt}")

                def ph_dv():
                    dv.waitm(f"g_pe_done{GQ-1}")
                    dv.waitm(f"rstd{GQ-1}")
                    dv.op("tensor_scalar_add", r1p[0:1, :], rsrow[0:1, :], 1.0)
                    dv.op("tensor_mul", r2p[0:1, :], mrow[0:1, :], rsrow[0:1, :])
                    dv.dmark("p2rows")
                    dv.waitm("p2bc")
                    dv.op("tensor_copy", onepb[:], psB2[:, 0:1024])
                    dv.mark("p2bc1")
                    dv.waitm("p2bc2m")
                    dv.op("tensor_copy", mrb[:], psB2[:, 0:1024])
                    dv.mark("p2bc2")
                    for dt in range(NDT):
                        dv.waitm(f"omm{dt}")
                        dv.waitm(f"xrl{dt}")
                        if dt >= 2:
                            dv.waitm(f"xaev{dt-2}")
                        sl = xaev[:, (dt % 2) * 1024:(dt % 2) * 1024 + 1024]
                        xsl = xt2[:, (dt % 2) * 1024:(dt % 2) * 1024 + 1024]
                        dv.op("tensor_mul", sl, xsl, onepb[:])
                        dv.op("tensor_sub", sl, sl, mrb[:])
                        dv.op("tensor_add", sl, sl, psO[dt % 2][:])
                        dv.op("drain", inc=1)
                        dv.mark(f"xadone{dt}")

                def ph_pe():
                    pe.waitm(f"g_dv_done{GQ-1}")
                    pe.waitm(f"rstd{GQ-1}")
                    pe.waitm("p2rows")
                    for piece in range(2):
                        pe.op("matmul", psB2[:, piece * 512:(piece + 1) * 512], ones_fr[:],
                              r1p[0:1, 3 + piece * 512: 3 + (piece + 1) * 512],
                              start=True, stop=True, inc=(1 if piece == 1 else 0))
                    pe.mark("p2bc")
                    pe.waitm("p2bc1")
                    for piece in range(2):
                        pe.op("matmul", psB2[:, piece * 512:(piece + 1) * 512], ones_fr[:],
                              r2p[0:1, 3 + piece * 512: 3 + (piece + 1) * 512],
                              start=True, stop=True, inc=(1 if piece == 1 else 0))
                    pe.mark("p2bc2m")
                    for dt in range(NDT):
                        pe.waitm(f"wo{dt}")
                        if dt == 0:
                            pe.waitm("rl_y")
                        if dt >= 2:
                            pe.waitm(f"xadone{dt-2}")
                        for f in range(64):
                            for tg in range(2):
                                pe.op("matmul", psO[dt % 2][:, tg * 512:(tg + 1) * 512],
                                      wo2[:, (dt % 2) * 8192 + f * 128:(dt % 2) * 8192 + (f + 1) * 128],
                                      ySB[:, f * 1024 + tg * 512: f * 1024 + (tg + 1) * 512],
                                      start=(f == 0), stop=(f == 63),
                                      inc=(1 if (f == 63 and tg == 1) else 0))
                        pe.mark(f"omm{dt}")

                _run_phase(ctx, nc, [
                    ("sync", sy, ph_sy), ("gpsimd", gp, ph_gp),
                    ("vector", dv, ph_dv), ("tensor", pe, ph_pe),
                ])

        if stage >= 9:
            with contextlib.ExitStack() as sP3:
                SB3 = lambda n, s, t: sP3.enter_context(nc.sbuf_tensor(n, s, t))
                xfT = SB3("xfT", [128, 32 * 1024], BF16)
                bw3 = SB3("bw3", [128, 2 * 4096], BF16)
                bp3 = SB3("bp3", [128, 2 * 1024], BF16)
                rhp = SB3("rhp", [128, 8 * 1024], BF16)
                rout3 = SB3("rout3", [128, 512], BF16)
                at3 = SB3("at3", [128, 2048], BF16)
                sq3 = SB3("sq3", [128, 1024], BF16)
                mb3 = SB3("mb3", [128, 1024], BF16)
                rs3 = SB3("rs3", [128, 1024], BF16)
                v3a = SB3("v3a", [128, 1024], F32)
                v3b = SB3("v3b", [128, 1024], F32)
                ce3 = SB3("ce3", [128, 16], F32)
                pr3 = SB3("pr3", [128, 16], F32)
                sc3 = SB3("sc3", [128, 8], F32)
                rhb = SB3("rhb", [128, 64], BF16)
                rhe = SB3("rhe", [128, 64], BF16)
                xa3 = SB3("xa3", [128, 2 * 1024], F32)
                ob3 = SB3("ob3", [128, 2 * 1024], F32)
                psOa = sP3.enter_context(nc.psum_tensor("psOa", [128, 1024], F32))
                psOb = sP3.enter_context(nc.psum_tensor("psOb", [128, 1024], F32))
                psRt = sP3.enter_context(nc.psum_tensor("psRt", [128, 512], F32))
                psR2x = sP3.enter_context(nc.psum_tensor("psR2x", [128, 512], F32))
                psT3 = sP3.enter_context(nc.psum_tensor("psT3", [128, 1024], BF16))
                psO3 = [psOa, psOb]

                def pi_gp():
                    gp.waitm(f"xaev{NDT-1}")
                    gp.waitm(f"omm{NDT-1}")
                    gp.waitm(f"xadone{NDT-1}")
                    gp.dma(rout3[:], wrout)
                    gp.dma(at3[:], wat)
                    for dt in range(NDT):
                        gp.dma(xfT[:, dt * 1024:(dt + 1) * 1024],
                               xab[dt * 128:(dt + 1) * 128, :])
                    gp.mark("i_xf")
                    for dt in range(NDT):
                        if dt >= 2:
                            gp.waitm(f"fmm{dt-2}")
                        gp.dma(bw3[:, (dt % 2) * 4096:(dt % 2) * 4096 + 4096],
                               wbase[dt * 128:(dt + 1) * 128, :])
                        gp.dma(bp3[:, (dt % 2) * 1024:(dt % 2) * 1024 + 1024],
                               wbp[dt * 128:(dt + 1) * 128, :])
                        gp.mark(f"bw{dt}")

                def pi_pe():
                    pe.waitm(f"xadone{NDT-1}")
                    pe.waitm(f"xaev{NDT-1}")
                    pe.waitm("i_xf")
                    for dt in range(NDT):
                        for tg in range(2):
                            pe.op("matmul", psOa[:, tg * 512:(tg + 1) * 512], ones_bb[:],
                                  xfT[:, dt * 1024 + tg * 512: dt * 1024 + (tg + 1) * 512],
                                  start=(dt == 0), stop=(dt == NDT - 1), inc=1)
                    pe.mark("i_sum")
                    for dt in range(NDT):
                        pe.waitm(f"i_sq{dt}")
                        for tg in range(2):
                            pe.op("matmul", psOb[:, tg * 512:(tg + 1) * 512], ones_bb[:],
                                  sq3[:, tg * 512:(tg + 1) * 512],
                                  start=(dt == 0), stop=(dt == NDT - 1), inc=1)
                        pe.mark(f"i_qmm{dt}")
                    pe.mark("i_sumsq")
                    for tt in range(8):
                        pe.waitm(f"i_xfok{NDT-1}")
                        if tt >= 1:
                            pe.waitm(f"i_rh_ev{tt-1}")
                            pe.waitm(f"i_sm{tt-1}")
                        for dt in range(NDT):
                            pe.op("matmul", psRt[:, 0:16],
                                  xfT[:, dt * 1024 + tt * 128: dt * 1024 + tt * 128 + 128],
                                  rout3[:, dt * 16:(dt + 1) * 16],
                                  start=(dt == 0), stop=(dt == NDT - 1), inc=0)
                            pe.op("matmul", psR2x[:, 0:64],
                                  xfT[:, dt * 1024 + tt * 128: dt * 1024 + tt * 128 + 128],
                                  at3[:, dt * 64:(dt + 1) * 64],
                                  start=(dt == 0), stop=(dt == NDT - 1),
                                  inc=(1 if dt == NDT - 1 else 0))
                        pe.mark(f"i_rt{tt}")
                        for e in range(16):
                            pe.waitm(f"i_rhe{tt}_{e}")
                            if tt * 16 + e >= 1:
                                pidx = tt * 16 + e - 1
                                pe.waitm(f"i_rhp{pidx // 16}_{pidx % 16}")
                            pe.op("transpose", psT3[0:64, 0:128], rhe[:], idbf[:], inc=1)
                            pe.mark(f"i_rtr{tt}_{e}")
                    for dt in range(NDT):
                        pe.waitm(f"bw{dt}")
                        if dt == 0:
                            pe.waitm(f"i_rhp7_15")
                        if dt >= 2:
                            pe.waitm(f"i_ob{dt-2}")
                        for d in range(NDT):
                            for tg in range(2):
                                pe.op("matmul", psO3[dt % 2][:, tg * 512:(tg + 1) * 512],
                                      bw3[:, (dt % 2) * 4096 + d * 128:(dt % 2) * 4096 + (d + 1) * 128],
                                      xfT[:, d * 1024 + tg * 512: d * 1024 + (tg + 1) * 512],
                                      start=(d == 0), stop=False, inc=0)
                        for pair in range(8):
                            for tg in range(2):
                                pe.op("matmul", psO3[dt % 2][:, tg * 512:(tg + 1) * 512],
                                      bp3[:, (dt % 2) * 1024 + pair * 128:(dt % 2) * 1024 + (pair + 1) * 128],
                                      rhp[:, pair * 1024 + tg * 512: pair * 1024 + (tg + 1) * 512],
                                      start=False, stop=(pair == 7),
                                      inc=(1 if (pair == 7 and tg == 1) else 0))
                        pe.mark(f"fmm{dt}")

                def pi_ac():
                    ac.waitm(f"xadone{NDT-1}")
                    ac.waitm(f"xaev{NDT-1}")
                    for dt in range(NDT):
                        if dt == 0:
                            ac.waitm("i_xf")
                        else:
                            ac.waitm(f"i_qmm{dt-1}")
                        ac.op("activation", sq3[:], xfT[:, dt * 1024:(dt + 1) * 1024], AF.Square)
                        ac.mark(f"i_sq{dt}")

                def pi_ac3():
                    for tt in range(8):
                        ac.waitm(f"i_nmax{tt}")
                        ac.op("activation", pr3[:], psRt[:, 0:16], AF.Exp, bias=sc3[:, 0:1])
                        ac.mark(f"i_exp{tt}")
                        ac.waitm(f"i_rt{tt}")
                        ac.op("copy", rhb[:], psR2x[:, 0:64])
                        ac.mark(f"i_rh_ev{tt}")

                def pi_dv():
                    dv.waitm(f"xaev{NDT-1}")
                    dv.waitm(f"omm{NDT-1}")
                    dv.waitm("i_sum")
                    dv.op("tensor_scalar_mul", v3a[:], psOa[:], 1.0 / DM)
                    dv.op("tensor_copy", mb3[:], v3a[:])
                    dv.mark("i_m")
                    dv.waitm("i_sumsq")
                    dv.op("tensor_scalar_mul", v3b[:], psOb[:], 1.0 / DM)
                    dv.op("tensor_mul", psOa[:], v3a[:], v3a[:]) if False else None
                    dv.op("tensor_mul", v3a[:], v3a[:], v3a[:])
                    dv.op("tensor_sub", v3b[:], v3b[:], v3a[:])
                    dv.mark("i_var")
                    dv.waitm("i_rstd")
                    dv.op("tensor_copy", rs3[:], v3b[:])
                    dv.mark("i_rs3")
                    for dt in range(NDT):
                        dv.op("tensor_sub", xfT[:, dt * 1024:(dt + 1) * 1024],
                              xfT[:, dt * 1024:(dt + 1) * 1024], mb3[:])
                        dv.op("tensor_mul", xfT[:, dt * 1024:(dt + 1) * 1024],
                              xfT[:, dt * 1024:(dt + 1) * 1024], rs3[:])
                        dv.op("drain", inc=1)
                        dv.mark(f"i_xfok{dt}")
                    for tt in range(8):
                        dv.waitm(f"i_rt{tt}")
                        dv.op("reduce_max", sc3[:, 0:1], psRt[:, 0:16], axis=AXX, negate=True)
                        dv.mark(f"i_nmax{tt}")
                        dv.waitm(f"i_exp{tt}")
                        dv.op("reduce_sum", sc3[:, 1:2], pr3[:], axis=AXX)
                        dv.op("reciprocal", sc3[:, 1:2], sc3[:, 1:2])
                        dv.op("tensor_scalar_mul", pr3[:], pr3[:], sc3[:, 1:2])
                        dv.op("reduce_max", sc3[:, 2:3], pr3[:], axis=AXX)
                        dv.op("tensor_scalar", ce3[:], pr3[:], sc3[:, 2:3], None, ALU.is_equal)
                        dv.op("scalar_tensor_tensor", v3a[:, 0:16], ce3[:], -1e30, pr3[:],
                              ALU.mult, ALU.add)
                        dv.op("reduce_max", sc3[:, 3:4], v3a[:, 0:16], axis=AXX)
                        dv.op("tensor_tensor", sc3[:, 4:5], sc3[:, 2:3], sc3[:, 3:4], ALU.add)
                        dv.op("tensor_scalar_add", sc3[:, 4:5], sc3[:, 4:5], 1e-6)
                        dv.op("reciprocal", sc3[:, 4:5], sc3[:, 4:5])
                        dv.op("tensor_scalar_mul", sc3[:, 4:5], sc3[:, 4:5], 0.5)
                        dv.op("tensor_tensor", sc3[:, 5:6], sc3[:, 2:3], sc3[:, 4:5], ALU.mult)
                        dv.op("tensor_tensor", sc3[:, 6:7], sc3[:, 3:4], sc3[:, 4:5], ALU.mult)
                        dv.op("tensor_scalar", v3a[:, 16:32], pr3[:], sc3[:, 3:4], None, ALU.is_equal)
                        dv.op("tensor_scalar_mul", ce3[:], ce3[:], sc3[:, 5:6])
                        dv.op("scalar_tensor_tensor", ce3[:], v3a[:, 16:32], sc3[:, 6:7], ce3[:],
                              ALU.mult, ALU.add)
                        dv.op("drain", inc=1)
                        dv.mark(f"i_sm{tt}")
                        for e in range(16):
                            dv.waitm(f"i_rh_ev{tt}")
                            if e >= 1:
                                dv.waitm(f"i_rtr{tt}_{e-1}")
                            dv.op("tensor_scalar_mul", rhe[:], rhb[:], ce3[:, e:e + 1])
                            dv.op("drain", inc=1)
                            dv.mark(f"i_rhe{tt}_{e}")
                            dv.waitm(f"i_rtr{tt}_{e}")
                            dv.op("tensor_copy",
                                  rhp[(e % 2) * 64:(e % 2) * 64 + 64, (e // 2) * 1024 + tt * 128:
                                      (e // 2) * 1024 + tt * 128 + 128],
                                  psT3[0:64, 0:128])
                            dv.mark(f"i_rhp{tt}_{e}")
                    for dt in range(NDT):
                        dv.waitm(f"fmm{dt}")
                        dv.waitm(f"i_xa{dt}")
                        osl = ob3[:, (dt % 2) * 1024:(dt % 2) * 1024 + 1024]
                        dv.op("tensor_scalar_add", osl, psO3[dt % 2][:], bbcol[:, dt:dt + 1])
                        dv.op("tensor_add", osl, osl, xa3[:, (dt % 2) * 1024:(dt % 2) * 1024 + 1024])
                        dv.op("drain", inc=1)
                        dv.mark(f"i_ob{dt}")

                def pi_ac2():
                    ac.waitm("i_var")
                    ac.op("activation", v3b[:], v3b[:], AF.Ln, bias=epscol[:])
                    ac.op("activation", v3b[:], v3b[:], AF.Exp, scale=-0.5)
                    ac.mark("i_rstd")

                def pi_sy():
                    sy.waitm(f"xadone{NDT-1}")
                    for dt in range(NDT):
                        if dt >= 2:
                            sy.waitm(f"i_ob{dt-2}")
                        sy.dma(xa3[:, (dt % 2) * 1024:(dt % 2) * 1024 + 1024],
                               xab[dt * 128:(dt + 1) * 128, :])
                        sy.mark(f"i_xa{dt}")
                        sy.waitm(f"i_ob{dt}")
                        sy.dma(outT[dt * 128:(dt + 1) * 128, :],
                               ob3[:, (dt % 2) * 1024:(dt % 2) * 1024 + 1024])
                        sy.mark(f"i_out{dt}")

                _run_phase(ctx, nc, [
                    ("gpsimd", gp, pi_gp),
                    ("scalar", ac, pi_ac),
                    ("scalar", ac, pi_ac2),
                    ("scalar", ac, pi_ac3),
                    ("tensor", pe, pi_pe),
                    ("vector", dv, pi_dv),
                    ("sync", sy, pi_sy),
                ])

        def fin_sy():
            if stage < 9:
                sy.dma(outT[0:128, 0:128], idf[:])
            sy.wait_queue_idle(sy)
            sy.wait_queue_idle(ac)
            sy.wait_queue_idle(gp)
            sy.op("nop", inc=0)

        _run_phase(ctx, nc, [("sync", sy, fin_sy)])

    return nc



_NC_CACHE = {}


def _get_program():
    if "nc" not in _NC_CACHE:
        _NC_CACHE["nc"] = build(stage=9, debug=False)
    return _NC_CACHE["nc"]


def kernel(**inputs):
    hp = _host_prep(inputs)
    nc = _get_program()
    shared = {k: hp[k] for k in ("win", "wout", "wbase", "wbp", "wat", "wrout", "nwcol",
                                 "cwt", "cbt", "dtb", "alog", "dpc", "gnc", "bbc", "idn",
                                 "trim")}
    in_maps = []
    for k in range(NC):
        m = dict(shared)
        m["xt"] = hp["xt"][k]
        m["gmt"] = hp["gmt"][k]
        m["mcol"] = hp["mcol"][k]
        in_maps.append(m)
    res = run_bass_kernel_spmd(nc, in_maps, core_ids=list(range(NC)))
    out = np.empty((2, 4096, DM), np.float32)
    for k in range(NC):
        b, q = k // 4, k % 4
        out[b, 1024 * q:1024 * (q + 1)] = np.asarray(res.results[k]["outT"]).T
    return out


# version 04 (final)
# speedup vs baseline: 1.0263x; 1.0263x over previous
import contextlib
import numpy as np
import ml_dtypes
import concourse.bass as bass
import concourse.mybir as mybir
from concourse.bass_utils import run_bass_kernel_spmd

F32 = mybir.dt.float32
BF16 = mybir.dt.bfloat16
AF = mybir.ActivationFunctionType
ALU = mybir.AluOpType
AXX = mybir.AxisListType.X

NC = 8
TOK = 1024
HALO = 3
TW = TOK + HALO
DM = 4096
DI = 8192
H, P, GQ, NS, CH, NCH = 128, 64, 8, 128, 128, 8
EPS = 1e-5
NDT = DM // 128
BF = ml_dtypes.bfloat16

FT_HID0, FT_B0, FT_C0, FT_DT = 64, 128, 136, 144
G_HBC = [(0, 344), (344, 344), (688, 339)]
G_GATE = [(3, 512), (515, 512)]
P0G = [(0, 512), (512, 512), (1024, 3)]
RG8 = [[0, 1, 2, 3, 4, 5, 6, 7]]


def _tile_lhsT(wT):
    K, M = wT.shape
    Kt, Mt = K // 128, M // 128
    r = wT.reshape(Kt, 128, Mt, 128).transpose(2, 1, 0, 3)
    return np.ascontiguousarray(r.reshape(Mt * 128, K))


def _host_prep(inputs):
    d = {}
    d["win"] = _tile_lhsT(np.ascontiguousarray(inputs["L0_attn_in_proj_w"].T))
    d["wout"] = _tile_lhsT(np.ascontiguousarray(inputs["L0_attn_out_proj_w"].T))
    d["wbase"] = _tile_lhsT(np.ascontiguousarray(inputs["L0_ffn_base_w"].T))
    BT = np.ascontiguousarray(inputs["L0_ffn_B"].transpose(0, 2, 1))
    Bp = np.concatenate([BT[0::2], BT[1::2]], axis=1)
    d["wbp"] = np.ascontiguousarray(
        Bp.reshape(8, 128, 32, 128).transpose(2, 1, 0, 3).reshape(32 * 128, 8 * 128))
    AT = np.ascontiguousarray(inputs["L0_ffn_A"].T)
    d["wat"] = np.ascontiguousarray(AT.reshape(32, 128, 64).transpose(1, 0, 2).reshape(128, 32 * 64))
    RT = np.ascontiguousarray(inputs["L0_ffn_router_w"].T)
    d["wrout"] = np.ascontiguousarray(RT.reshape(32, 128, 16).transpose(1, 0, 2).reshape(128, 32 * 16))
    d["nwcol"] = np.ascontiguousarray(inputs["L0_attn_norm_w"].reshape(32, 128).T)
    d["cwt"] = np.ascontiguousarray(
        inputs["L0_attn_conv_w"].reshape(80, 128, 4).transpose(1, 0, 2).reshape(128, 320))
    d["cbt"] = np.ascontiguousarray(inputs["L0_attn_conv_b"].reshape(80, 128).T)
    d["dtb"] = np.ascontiguousarray(inputs["L0_attn_dt_bias"].reshape(128, 1))
    d["alog"] = np.ascontiguousarray(inputs["L0_attn_A_log"].reshape(128, 1))
    d["dpc"] = np.ascontiguousarray(np.repeat(inputs["L0_attn_Dp"], 64).reshape(64, 128).T)
    d["gnc"] = np.ascontiguousarray(inputs["L0_attn_gnorm_w"].reshape(64, 128).T)
    d["bbc"] = np.ascontiguousarray(inputs["L0_ffn_base_b"].reshape(32, 128).T)
    d["idn"] = np.eye(128, dtype=np.float32)
    d["trim"] = np.triu(np.ones((128, 128), dtype=np.float32))
    x = inputs["L0_attn_x"]
    d["xt"], d["gmt"], d["mcol"] = [], [], []
    for k in range(NC):
        b, q = k // 4, k % 4
        t0 = 1024 * q
        xs = np.zeros((TW, DM), np.float32)
        lo = max(0, t0 - HALO)
        xs[HALO - (t0 - lo):] = x[b, lo:t0 + TOK]
        xT = np.ascontiguousarray(xs.T)
        d["xt"].append(np.ascontiguousarray(
            xT.reshape(32, 128, TW).transpose(1, 0, 2).reshape(128, 32 * TW)))
        g = np.zeros((8, 8), np.float32)
        m = np.zeros((8, 1), np.float32)
        for j in range(8):
            if j // 4 == b and j % 4 < q:
                m[j] = 1.0
                for i in range(8):
                    if i // 4 == b and (j % 4) < (i % 4) < q:
                        g[j, i] = 1.0
        gT = np.zeros((8, 128), np.float32)
        gT[:, 0:8] = g.T
        d["gmt"].append(gT)
        d["mcol"].append(m)
    return d


class Ctx:
    def __init__(self, nc):
        self.nc = nc
        self.dry = True
        self.sems = {}
        self.handles = {}
        self.ms = {}

    def sem(self, name):
        self.sems.setdefault(name, 0)
        return name


NSUB = 4


class Eng:
    def __init__(self, ctx, semname, auto_serial=False, dma_family=None):
        self.ctx = ctx
        self.sn = ctx.sem(semname)
        self.real = None
        self.auto = auto_serial
        self.fam = dma_family
        if dma_family:
            self.subs = [ctx.sem(f"{dma_family}{i}") for i in range(NSUB)]
        self.rr = 0
        self.last_dma = None
        self._last_was_dma = False

    def mark(self, name):
        if self.ctx.dry:
            assert name not in self.ctx.ms, f"dup milestone {name}"
            if self._last_was_dma:
                self.ctx.ms[name] = ("MULTI", tuple(
                    (s, self.ctx.sems[s]) for s in self.subs if self.ctx.sems[s] > 0))
            else:
                self.ctx.ms[name] = (self.sn, self.ctx.sems[self.sn])

    def waitm(self, ms_name):
        if not self.ctx.dry:
            sn, v = self.ctx.ms[ms_name]
            if sn == "MULTI":
                for s, c in v:
                    self.real.wait_ge(self.ctx.handles[s], c)
            else:
                self.real.wait_ge(self.ctx.handles[sn], v)

    def wait_count(self, semname, v):
        if not self.ctx.dry:
            self.real.wait_ge(self.ctx.handles[semname], v)

    def wait_queue_idle(self, eng):
        if not self.ctx.dry:
            for s in eng.subs:
                self.real.wait_ge(self.ctx.handles[s], self.ctx.sems[s])

    def op(self, fn, *args, inc=1, semname=None, **kw):
        sn = semname or self.sn
        if self.auto and not self.ctx.dry and self.ctx.sems[self.sn] > 0 \
                and fn not in ("dma_start",):
            self.real.wait_ge(self.ctx.handles[self.sn], self.ctx.sems[self.sn])
        if inc:
            self.ctx.sems[sn] += inc
        self._last_was_dma = False
        if self.ctx.dry:
            return None
        instr = getattr(self.real, fn)(*args, **kw)
        if inc:
            instr.then_inc(self.ctx.handles[sn], inc)
        return instr

    def dma(self, out, in_, **kw):
        s = self.subs[self.rr % NSUB]
        self.rr += 1
        if self.ctx.sems[s] > 0 and not self.ctx.dry:
            self.real.wait_ge(self.ctx.handles[s], self.ctx.sems[s])
        self.ctx.sems[s] += 16
        self.last_dma = (s, self.ctx.sems[s])
        self._last_was_dma = True
        if self.ctx.dry:
            return None
        instr = self.real.dma_start(out, in_, **kw)
        instr.then_inc(self.ctx.handles[s], 16)
        return instr

    def dmark(self, name):
        self.op("drain", inc=1)
        self.mark(name)


def _run_phase(ctx, nc, sections):
    snap = dict(ctx.sems)
    esnap = {id(p): (p.rr, p.last_dma, p._last_was_dma) for _, p, _ in sections}
    ctx.dry = True
    for _, _, fn in sections:
        fn()
    dry_end = dict(ctx.sems)
    dry_rr = {id(p): p.rr for _, p, _ in sections}
    ctx.sems.clear()
    ctx.sems.update(snap)
    for _, p, _ in sections:
        if id(p) in esnap:
            p.rr, p.last_dma, p._last_was_dma = esnap[id(p)]
    ctx.dry = False
    with nc.Block() as blk:
        for kind, proxy, fn in sections:
            def body(real, p=proxy, f=fn):
                p.real = real
                f()
            getattr(blk, kind)(body)
    assert ctx.sems == dry_end, (
        "count drift: " + str({k: (ctx.sems.get(k), dry_end.get(k))
                               for k in set(ctx.sems) | set(dry_end)
                               if ctx.sems.get(k) != dry_end.get(k)}))
    for _, p, _ in sections:
        assert p.rr == dry_rr[id(p)], f"rr drift {p.sn}"
    ctx.dry = True


NO_SIN = False
FAKE_CC = False
NO_GNORM = False
G_LIM = None


def build(stage=1, debug=False):
    nc = bass.Bass("TRN2", target_bir_lowering=False, debug=False)
    EI, EO = "ExternalInput", "ExternalOutput"
    DBG = EO if debug else "Internal"

    def din(name, shape, dt=F32, kind=EI):
        return nc.dram_tensor(name, shape, dt, kind=kind).ap()

    xt = din("xt", [128, 32 * TW])
    win = din("win", [145 * 128, DM])
    wout = din("wout", [32 * 128, DI])
    wbase = din("wbase", [32 * 128, DM])
    wbp = din("wbp", [32 * 128, 8 * 128])
    wat = din("wat", [128, 32 * 64])
    wrout = din("wrout", [128, 32 * 16])
    nwcol = din("nwcol", [128, 32])
    cwt = din("cwt", [128, 320])
    cbt = din("cbt", [128, 80])
    dtb = din("dtb", [128, 1])
    alog = din("alog", [128, 1])
    dpc = din("dpc", [128, 64])
    gnc = din("gnc", [128, 64])
    bbc = din("bbc", [128, 32])
    idn = din("idn", [128, 128])
    trim = din("trim", [128, 128])
    gmt = din("gmt", [8, 128])
    mcolA = din("mcol", [8, 1])

    hidb = din("hidb", [64 * 128, 1024], BF16, DBG)
    hcb = din("hcb", [8 * 128, 8192], BF16, DBG)
    bcb = din("bcb", [16 * 128, 1024], BF16, DBG)
    gateb = din("gateb", [64 * 128, 1024], BF16, DBG)
    pclb = din("pclb", [64 * 128, 1024], F32, DBG)
    acsb = din("acsb", [128, 1024], F32, DBG)
    dtd = din("dtd", [128, 1024], F32, DBG)
    offsd = din("offsd", [128, 16], F32, "Internal")
    ccin = din("ccin", [128, 8192], F32, "Internal")
    ccin2 = din("ccin2", [1, 128], F32, "Internal")
    ccout2 = din("ccout2", [8, 128], F32, "Internal")
    ccout = din("ccout", [1024, 8192], F32, "Internal")
    wscr = din("wscr", [8, 128], F32, "Internal")
    sind = din("sind", [128, 8192], F32, DBG)
    ybuf = din("ybuf", [64 * 128, 1024], BF16, DBG)
    xab = din("xab", [32 * 128, 1024], F32, DBG)
    outT = nc.dram_tensor("outT", [DM, 1024], F32, kind=EO).ap()

    ctx = Ctx(nc)
    pe = Eng(ctx, "TP")
    dv = Eng(ctx, "TV", auto_serial=True)
    ac = Eng(ctx, "TA", auto_serial=True, dma_family="QA")
    gp = Eng(ctx, "QG", dma_family="QG")
    sy = Eng(ctx, "QS", dma_family="QS")
    ctx.sem("CK")

    with contextlib.ExitStack() as stk:
        for sname in list(ctx.sems):
            ctx.handles[sname] = stk.enter_context(nc.semaphore(sname))

        SB = lambda n, s, t: stk.enter_context(nc.sbuf_tensor(n, s, t))

        ones_bc = SB("ones_bc", [128, 1], BF16)
        ones_bb = SB("ones_bb", [128, 128], BF16)
        ones_fr = SB("ones_fr", [1, 128], F32)
        idbf = SB("idbf", [128, 128], BF16)
        trimask = SB("trimask", [128, 128], BF16)
        idf = SB("idf", [128, 128], F32)
        nwc = SB("nwc", [128, 32], F32)
        cw = SB("cw", [128, 320], F32)
        cbS = SB("cbS", [128, 80], F32)
        dtbias = SB("dtbias", [128, 1], F32)
        negA = SB("negA", [128, 1], F32)
        dpcol = SB("dpcol", [128, 64], F32)
        gncol = SB("gncol", [128, 64], F32)
        bbcol = SB("bbcol", [128, 32], F32)
        epscol = SB("epscol", [128, 1], F32)
        onecol = SB("onecol", [128, 1], F32)
        mrow = SB("mrow", [1, TW], F32)
        rsrow = SB("rsrow", [1, TW], F32)
        sDT = contextlib.ExitStack()
        SBD = lambda n, s, t: sDT.enter_context(nc.sbuf_tensor(n, s, t, side="right"))
        acs_tok = SBD("acs_tok", [128, NCH * 128], F32)
        offs = SBD("offs", [128, 16], F32)
        sDT2 = contextlib.ExitStack()
        SBD2 = lambda n, s, t: sDT2.enter_context(nc.sbuf_tensor(n, s, t, side="right"))

        def c_gp():
            gp.dma(trimask[:], trim)
            gp.dma(idbf[:], idn)
            gp.dma(cw[:], cwt)
            gp.mark("c_qg")

        def c_sy():
            sy.dma(idf[:], idn)
            sy.dma(nwc[:], nwcol)
            sy.dma(cbS[:], cbt)
            sy.dma(dtbias[:], dtb)
            sy.dma(negA[:], alog)
            sy.dma(dpcol[:], dpc)
            sy.dma(gncol[:], gnc)
            sy.dma(bbcol[:], bbc)
            sy.mark("c_qs")

        def c_dv():
            dv.op("memset", ones_bc[:], 1.0)
            dv.op("memset", ones_bb[:], 1.0)
            dv.op("memset", ones_fr[:], 1.0)
            dv.op("memset", epscol[:], EPS)
            dv.op("memset", onecol[:], 1.0)
            dv.mark("c_ones")

        def c_ac():
            ac.waitm("c_qs")
            ac.op("activation", negA[:], negA[:], AF.Exp)
            ac.mark("c_expA")

        def c_dv2():
            dv.waitm("c_expA")
            dv.op("tensor_scalar_mul", negA[:], negA[:], -1.0)
            dv.mark("c_negA")

        _run_phase(ctx, nc, [
            ("gpsimd", gp, c_gp), ("sync", sy, c_sy), ("vector", dv, c_dv),
            ("scalar", ac, c_ac), ("vector", dv, c_dv2),
        ])

        sSin = contextlib.ExitStack()
        S_in_bf = sSin.enter_context(nc.sbuf_tensor("S_in_bf", [128, 8192], BF16, side="right"))
        sH = contextlib.ExitStack()
        hT = sH.enter_context(nc.sbuf_tensor("hT", [128, 32 * TW], BF16, side="right"))
        dt_tok = SBD2("dt_tok", [128, NCH * 128], BF16)
        dc_all = SBD2("dc_all", [128, NCH * 128], BF16)
        expa = SBD2("expa", [128, NCH * 128], F32)

        with contextlib.ExitStack() as s0:
            SB0 = lambda n, s, t: s0.enter_context(nc.sbuf_tensor(n, s, t))
            PS0 = lambda n, s, t: s0.enter_context(nc.psum_tensor(n, s, t))
            sqt = SB0("sqt", [128, TW], BF16)
            mbb = SB0("mbb", [128, TW], BF16)
            csbuf = SB0("csbuf", [128, TW], BF16)
            vf = SB0("vf", [128, TW], F32)
            wf = SB0("wf", [128, TW], F32)
            t2 = SB0("t2", [128, TW], F32)
            ps_r = [PS0(f"ps_r{i}", [128, 512], F32) for i in range(3)]

            def p0_gp():
                gp.dma(hT[:], xt)
                gp.mark("p0_x")

            def p0_pe():
                pe.waitm("p0_x")
                pe.waitm("c_ones")
                for gi, (c0, n) in enumerate(P0G):
                    for dt in range(NDT):
                        last = (gi == 2 and dt == NDT - 1)
                        pe.op("matmul", ps_r[gi][:, 0:n], ones_bb[:],
                              hT[:, dt * TW + c0: dt * TW + c0 + n],
                              start=(dt == 0), stop=(dt == NDT - 1), inc=1 if last else 0)
                pe.mark("p0_sum")
                for dt in range(NDT):
                    pe.waitm(f"p0_sq{dt}")
                    if dt == 0:
                        pe.waitm("p0_m")
                    for gi, (c0, n) in enumerate(P0G):
                        pe.op("matmul", ps_r[gi][:, 0:n], ones_bb[:],
                              sqt[:, c0:c0 + n],
                              start=(dt == 0), stop=(dt == NDT - 1),
                              inc=1 if gi == 2 else 0)
                    pe.mark(f"p0_qmm{dt}")
                pe.mark("p0_sumsq")
                for dt in range(NDT):
                    pe.waitm(f"p0_sq2_{dt}")
                    if dt == 0:
                        pe.waitm("p0_var")
                    for gi, (c0, n) in enumerate(P0G):
                        pe.op("matmul", ps_r[gi][:, 0:n], ones_bb[:],
                              sqt[:, c0:c0 + n],
                              start=(dt == 0), stop=(dt == NDT - 1),
                              inc=1 if gi == 2 else 0)
                    pe.mark(f"p0_q2mm{dt}")
                pe.mark("p0_sumsq2")

            def p0_dv():
                dv.waitm("p0_sum")
                for gi, (c0, n) in enumerate(P0G):
                    dv.op("tensor_scalar_mul", vf[:, c0:c0 + n], ps_r[gi][:, 0:n], 1.0 / DM)
                dv.op("tensor_copy", mbb[:], vf[:])
                dv.op("tensor_copy", mrow[0:1, :], vf[0:1, :])
                dv.mark("p0_m")
                dv.waitm("p0_sumsq")
                for gi, (c0, n) in enumerate(P0G):
                    dv.op("tensor_scalar_mul", wf[:, c0:c0 + n], ps_r[gi][:, 0:n], 1.0 / DM)
                dv.op("tensor_mul", t2[:], vf[:], vf[:])
                dv.op("tensor_sub", wf[:], wf[:], t2[:])
                dv.mark("p0_var")
                dv.waitm(f"p0_sq{NDT-1}")
                for dt in range(NDT):
                    dv.op("tensor_sub", hT[:, dt * TW:(dt + 1) * TW],
                          hT[:, dt * TW:(dt + 1) * TW], mbb[:])
                dv.mark("p0_h1")
                dv.waitm("p0_rstd")
                dv.op("tensor_copy", rsrow[0:1, :], wf[0:1, :])
                dv.mark("p0_rsrow")
                dv.waitm("p0_sumsq2")
                for gi, (c0, n) in enumerate(P0G):
                    dv.op("tensor_scalar_mul", t2[:, c0:c0 + n], ps_r[gi][:, 0:n], 1.0 / DM)
                dv.op("tensor_mul", t2[:], t2[:], wf[:])
                dv.op("tensor_mul", t2[:], t2[:], wf[:])
                dv.mark("p0_rms2")
                dv.waitm("p0_rs2")
                dv.op("tensor_mul", t2[:], t2[:], wf[:])
                dv.op("tensor_copy", csbuf[:], t2[:])
                dv.mark("p0_cs")
                dv.waitm(f"p0_sq2_{NDT-1}")
                for dt in range(NDT):
                    dv.op("scalar_tensor_tensor", hT[:, dt * TW:(dt + 1) * TW],
                          hT[:, dt * TW:(dt + 1) * TW], nwc[:, dt:dt + 1], csbuf[:],
                          ALU.mult, ALU.mult)
                dv.mark("p0_hT")

            def p0_ac():
                for dt in range(NDT):
                    if dt == 0:
                        ac.waitm("p0_x")
                    else:
                        ac.waitm(f"p0_qmm{dt-1}")
                    ac.op("activation", sqt[:], hT[:, dt * TW:(dt + 1) * TW], AF.Square)
                    ac.mark(f"p0_sq{dt}")
                ac.waitm("p0_var")
                ac.op("activation", wf[:], wf[:], AF.Ln, bias=epscol[:])
                ac.op("activation", wf[:], wf[:], AF.Exp, scale=-0.5)
                ac.mark("p0_rstd")
                for dt in range(NDT):
                    if dt == 0:
                        ac.waitm("p0_h1")
                    else:
                        ac.waitm(f"p0_q2mm{dt-1}")
                    ac.op("activation", sqt[:], hT[:, dt * TW:(dt + 1) * TW], AF.Square)
                    ac.mark(f"p0_sq2_{dt}")
                ac.waitm("p0_rms2")
                ac.op("activation", t2[:], t2[:], AF.Ln, bias=epscol[:])
                ac.op("activation", t2[:], t2[:], AF.Exp, scale=-0.5)
                ac.mark("p0_rs2")

            if stage >= 1:
                _run_phase(ctx, nc, [
                    ("gpsimd", gp, p0_gp),
                    ("tensor", pe, p0_pe),
                    ("vector", dv, p0_dv),
                    ("scalar", ac, p0_ac),
                ])

        with contextlib.ExitStack() as s1:
            SB1 = lambda n, s, t: s1.enter_context(nc.sbuf_tensor(n, s, t))
            PS1 = lambda n, s, t: s1.enter_context(nc.psum_tensor(n, s, t))
            wbuf = SB1("wbuf", [128, 2 * DM], BF16)
            sE1 = contextlib.ExitStack()
            SBE = lambda n, s, t: sE1.enter_context(nc.sbuf_tensor(n, s, t))
            convsrc = SBE("convsrc", [128, 2 * TW], BF16)
            convacc = SBE("convacc", [128, 2 * 1024], BF16)
            hidg = SBE("hidg", [128, 2 * 8 * 1024], BF16)
            hcg = SBE("hcg", [128, 8 * 1024], BF16)
            BgS = SBE("BgS", [128, 2 * 1024], BF16)
            CgS = SBE("CgS", [128, 2 * 1024], BF16)
            dtTf = SBE("dtTf", [128, 1024], F32)
            zf = SBE("zf", [128, 1024], F32)
            acsT = SBE("acsT", [128, 1024], F32)
            zrow = SBE("zrow", [128, 128], F32)
            Pg = SBE("Pg", [128, 2 * 1024], F32)
            hcd_t = SBE("hcd_t", [128, 2 * 128], BF16)
            btokS = SBE("btokS", [128, 2 * 128], BF16)
            dcT_t = SBE("dcT_t", [128, 128], BF16)
            eaT_t = SBE("eaT_t", [128, 128], F32)

            ps_in = [PS1(f"ps_in{i}", [128, 512], F32) for i in range(5)]
            BANKS_EVEN = [ps_in[0], ps_in[1], ps_in[2]]
            BANKS_ODD = [ps_in[3], ps_in[4], ps_in[2]]
            sCD = contextlib.ExitStack()
            ps_trf_full = sCD.enter_context(nc.psum_tensor("ps_trf", [128, 512], F32))
            ps_trf = ps_trf_full[:, 0:128]
            ps_trb_cd = sCD.enter_context(nc.psum_tensor("ps_trb_cd", [128, 1024], BF16))[:, 0:128]

            def pc_gp():
                gp.waitm("p0_hT")
                gp.waitm("p0_sumsq2")
                gp.dma(wbuf[:, 0:DM], win[FT_DT * 128:(FT_DT + 1) * 128, :])
                gp.mark("w_dt")

            def pc_pe():
                pe.waitm("w_dt")
                pe.waitm("p0_hT")
                for dt in range(NDT):
                    for gi, (c0, n) in enumerate(G_HBC):
                        last = (gi == 2 and dt == NDT - 1)
                        pe.op("matmul", BANKS_EVEN[gi][:, 0:n],
                              wbuf[:, dt * 128:(dt + 1) * 128],
                              hT[:, dt * TW + c0: dt * TW + c0 + n],
                              start=(dt == 0), stop=(dt == NDT - 1), inc=1 if last else 0)
                pe.mark("mm_dt")

            def pc_ac():
                ac.waitm("mm_dt")
                offsets = [0, 344, 688]
                for gi in (2, 0, 1):
                    c0, n = G_HBC[gi]
                    ac.op("activation", convsrc[:, offsets[gi]:offsets[gi] + n],
                          BANKS_EVEN[gi][:, 0:n], AF.Identity, bias=dtbias[:])
                    if gi == 2:
                        ac.mark("ev2_dt")
                ac.mark("ev_dt")

            if stage >= 2:
                _run_phase(ctx, nc, [
                    ("gpsimd", gp, pc_gp), ("tensor", pe, pc_pe), ("scalar", ac, pc_ac),
                ])

            def pd_dv():
                z = convsrc[:, 3:3 + 1024]
                dv.op("memset", zrow[:], 0.0)
                dv.waitm("ev_dt")
                dv.op("tensor_scalar_max", dtTf[:], z, 0.0)
                dv.mark("dt_relu")
                dv.waitm("dt_absln")
                dv.op("tensor_add", dtTf[:], dtTf[:], zf[:])
                dv.dmark("dt_done")
                dv.waitm("c_negA")
                dv.op("tensor_scalar_mul", zf[:], dtTf[:], negA[:])
                dv.mark("adt")
                for c in range(NCH):
                    dv.op("tensor_tensor_scan", acsT[:, c * 128:(c + 1) * 128],
                          zf[:, c * 128:(c + 1) * 128], zrow[:], 0.0, ALU.add, ALU.add)
                dv.dmark("acs")
                dv.op("memset", offs[:], 0.0)
                for c in range(NCH):
                    dv.op("tensor_add", offs[:, c + 1:c + 2], offs[:, c:c + 1],
                          acsT[:, c * 128 + 127:c * 128 + 128])
                dv.dmark("offs")
                for c in range(NCH):
                    dv.waitm(f"cp_dctr{c}")
                    dv.op("tensor_copy", dc_all[:, c * 128:(c + 1) * 128], ps_trb_cd)
                    dv.mark(f"cp_dc{c}")

            def pd_ac():
                z = convsrc[:, 3:3 + 1024]
                ac.waitm("dt_relu")
                ac.op("activation", zf[:], z, AF.Abs)
                ac.op("activation", zf[:], zf[:], AF.Exp, scale=-1.0)
                ac.op("activation", zf[:], zf[:], AF.Ln, bias=onecol[:])
                ac.mark("dt_absln")
                for c in range(NCH):
                    acslice = acsT[:, c * 128:(c + 1) * 128]
                    alast = acsT[:, c * 128 + 127:c * 128 + 128]
                    ac.waitm(f"cp_dtt{c}")
                    ac.op("copy", dt_tok[:, c * 128:(c + 1) * 128], ps_trf)
                    ac.mark(f"cp_dtev{c}")
                    ac.waitm(f"cp_acst{c}")
                    ac.op("copy", acs_tok[:, c * 128:(c + 1) * 128], ps_trf)
                    ac.mark(f"cp_ev{c}")
                    if c >= 1:
                        ac.waitm(f"cp_dctr{c-1}")
                    ac.op("activation", dcT_t[:], acslice, AF.Exp, scale=-1.0, bias=alast)
                    ac.mark(f"cp_dcT{c}")
                    if c >= 1:
                        ac.waitm(f"cp_eat{c-1}")
                    ac.op("activation", eaT_t[:], acslice, AF.Exp, scale=0.0, bias=alast)
                    ac.mark(f"cp_eaT{c}")
                    ac.waitm(f"cp_eat{c}")
                    ac.op("copy", expa[:, c * 128:(c + 1) * 128], ps_trf)
                    ac.mark(f"cp_expa{c}")

            def pd_pe():
                for c in range(NCH):
                    if c == 0:
                        pe.waitm("dt_done")
                        pe.waitm("acs")
                    else:
                        pe.waitm(f"cp_expa{c-1}")
                    pe.op("transpose", ps_trf, dtTf[:, c * 128:(c + 1) * 128], idf[:], inc=1)
                    pe.mark(f"cp_dtt{c}")
                    pe.waitm(f"cp_dtev{c}")
                    pe.op("transpose", ps_trf, acsT[:, c * 128:(c + 1) * 128], idf[:], inc=1)
                    pe.mark(f"cp_acst{c}")
                    pe.waitm(f"cp_dcT{c}")
                    if c >= 1:
                        pe.waitm(f"cp_dc{c-1}")
                    pe.op("transpose", ps_trb_cd, dcT_t[:], idbf[:], inc=1)
                    pe.mark(f"cp_dctr{c}")
                    pe.waitm(f"cp_eaT{c}")
                    pe.waitm(f"cp_ev{c}")
                    pe.op("transpose", ps_trf, eaT_t[:], idf[:], inc=1)
                    pe.mark(f"cp_eat{c}")

            def pd_gp2():
                gp.waitm(f"cp_dc{NCH-1}")
                gp.dma(sind[:, 1024:2048], dc_all[:])
                gp.waitm(f"cp_dtev{NCH-1}")
                gp.dma(sind[:, 3072:4096], dt_tok[:])

            def pd_sy():
                sy.waitm("acs")
                sy.dma(acsb, acsT[:])
                sy.waitm("dt_done")
                sy.dma(dtd, dtTf[:])
                sy.waitm("offs")
                sy.dma(offsd, offs[:])
                sy.mark("offsd_done")
                sy.waitm("offsd_done")
                sy.waitm(f"cp_expa{NCH-1}")
                sy.dma(sind[:, 0:1024], expa[:])
                sy.waitm(f"cp_ev{NCH-1}")
                sy.dma(sind[:, 2048:3072], acs_tok[:])
                sy.dma(ccin2, offsd[:, 8:9].rearrange("h one -> (one h)").unsqueeze(0),
                      allow_slow_non_contiguous=True)
                sy.mark("dt_spill")

            if stage >= 3:
                _run_phase(ctx, nc, [
                    ("vector", dv, pd_dv), ("scalar", ac, pd_ac),
                    ("tensor", pe, pd_pe), ("sync", sy, pd_sy),
                    ("gpsimd", gp, pd_gp2),
                ])

            sCD.close()
            sE2 = contextlib.ExitStack()
            ps_tr2 = [sE2.enter_context(nc.psum_tensor(f"ps_trb{i}", [128, 1024], BF16))[:, 0:128] for i in range(2)]
            ps_sc = sE2.enter_context(nc.psum_tensor("ps_sc", [128, 512], F32))[:, 0:128]

            p1a_tiles = []
            for g in range(GQ):
                p1a_tiles += [FT_HID0 + g * 8 + i for i in range(8)] + [FT_B0 + g, FT_C0 + g]

            def tile_meta(ft):
                if ft < FT_B0:
                    g, kind, fco = (ft - FT_HID0) // 8, "h", ft - FT_HID0
                elif ft < FT_C0:
                    g, kind, fco = ft - FT_B0, "b", 64 + (ft - FT_B0)
                else:
                    g, kind, fco = ft - FT_C0, "c", 72 + (ft - FT_C0)
                return g, kind, fco

            def pe_gp():
                for i, ft in enumerate(p1a_tiles):
                    slot = i % 2
                    if i == 0:
                        gp.waitm("mm_dt")
                    if i >= 2:
                        gp.waitm(f"mm{p1a_tiles[i-2]}")
                    gp.dma(wbuf[:, slot * DM:(slot + 1) * DM], win[ft * 128:(ft + 1) * 128, :])
                    gp.mark(f"w{ft}")

            def pe_pe():
                for i, ft in enumerate(p1a_tiles):
                    slot = i % 2
                    bank3 = BANKS_EVEN if (i % 2 == 1) else BANKS_ODD
                    pe.waitm(f"w{ft}")
                    if i == 0:
                        pe.waitm("ev2_dt")
                    elif i == 1:
                        pe.waitm("ev_dt")
                        pe.waitm(f"ev2{p1a_tiles[0]}")
                    if i >= 2:
                        pe.waitm(f"ev{p1a_tiles[i-2]}")
                        pe.waitm(f"ev2{p1a_tiles[i-1]}")
                    for dt in range(NDT):
                        for gi, (c0, n) in enumerate(G_HBC):
                            last = (gi == 2 and dt == NDT - 1)
                            pe.op("matmul", bank3[gi][:, 0:n],
                                  wbuf[:, slot * DM + dt * 128: slot * DM + (dt + 1) * 128],
                                  hT[:, dt * TW + c0: dt * TW + c0 + n],
                                  start=(dt == 0), stop=(dt == NDT - 1), inc=1 if last else 0)
                    pe.mark(f"mm{ft}")
                    if stage >= 5:
                        g, kind, _ = tile_meta(ft)
                        if kind == "c" and g >= 1:
                            states_pe(g - 1)
                if stage >= 5:
                    states_pe(GQ - 1)

            def states_pe(g):
                sched = []
                for c in range(NCH):
                    sched.append(("b", c, 0))
                    for fb in range(8):
                        sched.append(("h", c, fb))
                for i, (kind, c, fb) in enumerate(sched):
                    slot = (g * 72 + i) % 2
                    gi2 = g * 72 + i - 2
                    if gi2 >= 0:
                        g2, i2 = gi2 // 72, gi2 % 72
                        k2, c2, fb2 = ("b", i2 // 9, 0) if i2 % 9 == 0 else ("h", i2 // 9, i2 % 9 - 1)
                        pe.waitm(f"st_rdb{g2}_{c2}" if k2 == "b" else f"st_rdh{g2}_{c2}_{fb2}")
                    if kind == "b":
                        if c == 0:
                            pe.waitm(f"silu{FT_B0 + g}")
                        pe.op("transpose", ps_tr2[slot][:],
                              BgS[:, (g % 2) * 1024 + c * 128:(g % 2) * 1024 + (c + 1) * 128],
                              idbf[:], inc=1)
                        pe.mark(f"st_btr{g}_{c}")
                    else:
                        if c == 0 and fb < 8:
                            pe.waitm(f"silu{FT_HID0 + g * 8 + fb}")
                        pe.op("transpose", ps_tr2[slot][:],
                              hidg[:, (g % 2) * 8192 + fb * 1024 + c * 128:
                                    (g % 2) * 8192 + fb * 1024 + (c + 1) * 128],
                              idbf[:], inc=1)
                        pe.mark(f"st_htr{g}_{c}_{fb}")
                        pe.waitm(f"st_hcd{g}_{c}_{fb}")
                        prevsc = (g, c, fb - 1) if fb >= 1 else ((g, c - 1, 7) if c >= 1 else
                                                                ((g - 1, NCH - 1, 7) if g >= 1 else None))
                        if prevsc is not None:
                            pe.waitm("st_pupd%d_%d_%d" % prevsc)
                        pe.op("matmul", ps_sc[:],
                              btokS[:, (c % 2) * 128:(c % 2) * 128 + 128],
                              hcd_t[:, (fb % 2) * 128:(fb % 2) * 128 + 128],
                              start=True, stop=True, inc=1)
                        pe.mark(f"st_sc{g}_{c}_{fb}")

            def pe_ac():
                for i, ft in enumerate(p1a_tiles):
                    slot = i % 2
                    bank3 = BANKS_EVEN if (i % 2 == 1) else BANKS_ODD
                    cs = convsrc[:, slot * TW:(slot + 1) * TW]
                    g, kind, fco = tile_meta(ft)
                    ac.waitm(f"mm{ft}")
                    if i == 0:
                        ac.waitm("dt_relu")
                        ac.waitm("dt_absln")
                    if i >= 2:
                        ac.waitm(f"taps{p1a_tiles[i-2]}")
                    offsets = [0, 344, 688]
                    for gi in (2, 0, 1):
                        c0, n = G_HBC[gi]
                        ac.op("copy", cs[:, offsets[gi]:offsets[gi] + n], bank3[gi][:, 0:n])
                        if gi == 2:
                            ac.mark(f"ev2{ft}")
                    ac.mark(f"ev{ft}")
                    ac.waitm(f"taps{ft}")
                    if kind == "h":
                        fb = fco % 8
                        dest = hidg[:, (g % 2) * 8192 + fb * 1024:(g % 2) * 8192 + (fb + 1) * 1024]
                        if g >= 2 and fb == 0:
                            ac.waitm(f"hspill{g-2}")
                            if stage >= 5:
                                ac.waitm(f"st_htr{g-2}_{NCH-1}_7")
                    elif kind == "b":
                        dest = BgS[:, (g % 2) * 1024:(g % 2) * 1024 + 1024]
                        if g >= 2:
                            ac.waitm(f"bspill{g-2}")
                            if stage >= 5:
                                ac.waitm(f"st_btr{g-2}_{NCH-1}")
                    else:
                        dest = CgS[:, (g % 2) * 1024:(g % 2) * 1024 + 1024]
                        if g >= 2:
                            ac.waitm(f"cspill{g-2}")
                    ac.op("activation", dest, convacc[:, slot * 1024:(slot + 1) * 1024],
                          AF.Silu, bias=cbS[:, fco:fco + 1])
                    ac.dmark(f"silu{ft}")

            def pe_dv():
                for i, ft in enumerate(p1a_tiles):
                    slot = i % 2
                    cs = convsrc[:, slot * TW:(slot + 1) * TW]
                    ca = convacc[:, slot * 1024:(slot + 1) * 1024]
                    g, kind, fco = tile_meta(ft)
                    dv.waitm(f"ev{ft}")
                    if i >= 2:
                        dv.waitm(f"silu{p1a_tiles[i-2]}")
                    dv.op("tensor_scalar_mul", ca[:], cs[:, 0:1024], cw[:, fco * 4:fco * 4 + 1])
                    for tp_ in (1, 2, 3):
                        dv.op("scalar_tensor_tensor", ca[:], cs[:, tp_:tp_ + 1024],
                              cw[:, fco * 4 + tp_:fco * 4 + tp_ + 1], ca[:], ALU.mult, ALU.add)
                    dv.mark(f"taps{ft}")
                    if stage >= 5:
                        if kind == "c" and g >= 1:
                            states_dv(g - 1)
                if stage >= 5:
                    states_dv(GQ - 1)

            def states_dv(g):
                for c in range(NCH):
                    dv.waitm(f"cp_dc{c}")
                    dv.waitm(f"cp_dtev{c}")
                    dv.waitm(f"cp_expa{c}")
                    dv.waitm(f"st_btr{g}_{c}")
                    slot_b = (g * 72 + c * 9) % 2
                    if c >= 2:
                        dv.waitm(f"st_sc{g}_{c-2}_7")
                    elif g >= 1:
                        dv.waitm(f"st_sc{g-1}_{NCH-2+c}_7")
                    dv.op("tensor_copy", btokS[:, (c % 2) * 128:(c % 2) * 128 + 128],
                          ps_tr2[slot_b][:])
                    dv.mark(f"st_rdb{g}_{c}")
                    for fb in range(8):
                        i = c * 9 + 1 + fb
                        slot = (g * 72 + i) % 2
                        dv.waitm(f"st_htr{g}_{c}_{fb}")
                        if c == 0 and fb == 0 and g >= 1:
                            dv.waitm(f"hcspill{g-1}")
                        dtr = dt_tok[:, c * 128 + g * 16 + 2 * fb: c * 128 + g * 16 + 2 * fb + 2]
                        dtr = dtr.unsqueeze(2).broadcast_to([128, 2, 64])
                        hcdst = hcg[:, c * 1024 + fb * 128: c * 1024 + (fb + 1) * 128]
                        dv.op("tensor_tensor", hcdst.rearrange("p (a b) -> p a b", a=2),
                              ps_tr2[slot][:].rearrange("p (a b) -> p a b", a=2), dtr, ALU.mult)
                        dv.mark(f"st_rdh{g}_{c}_{fb}")
                        prev2 = (g, c, fb - 2) if fb >= 2 else ((g, c - 1, 6 + fb) if c >= 1 else
                                                               ((g - 1, NCH - 1, 6 + fb) if g >= 1 else None))
                        if prev2 is not None:
                            dv.waitm("st_sc%d_%d_%d" % prev2)
                        dcr = dc_all[:, c * 128 + g * 16 + 2 * fb: c * 128 + g * 16 + 2 * fb + 2]
                        dcr = dcr.unsqueeze(2).broadcast_to([128, 2, 64])
                        dv.op("tensor_tensor",
                              hcd_t[:, (fb % 2) * 128:(fb % 2) * 128 + 128].rearrange("p (a b) -> p a b", a=2),
                              hcdst.rearrange("p (a b) -> p a b", a=2), dcr, ALU.mult)
                        dv.mark(f"st_hcd{g}_{c}_{fb}")
                        dv.waitm(f"st_sc{g}_{c}_{fb}")
                        if c == 0:
                            if fb == 0:
                                if g >= 1:
                                    dv.waitm(f"psnap{g-1}_7")
                        elif c == 1 and fb == 0 and g >= 1:
                            dv.waitm(f"ccin{g-1}")
                        if c >= 2 and fb == 0:
                            dv.waitm(f"psnap{g}_{c-1}")
                        for hh in range(2):
                            hcol = (2 * fb + hh) * 64
                            cur = Pg[:, (c % 2) * 1024 + hcol:(c % 2) * 1024 + hcol + 64]
                            nxt = Pg[:, ((c + 1) % 2) * 1024 + hcol:((c + 1) % 2) * 1024 + hcol + 64]
                            if c == 0:
                                dv.op("tensor_copy", nxt, ps_sc[:, hh * 64:hh * 64 + 64])
                            else:
                                dv.op("scalar_tensor_tensor", nxt, cur,
                                      expa[:, c * 128 + g * 16 + 2 * fb + hh:
                                           c * 128 + g * 16 + 2 * fb + hh + 1],
                                      ps_sc[:, hh * 64:hh * 64 + 64], ALU.mult, ALU.add)
                        if fb == 7:
                            dv.dmark(f"st_pupd{g}_{c}_{fb}")
                        else:
                            dv.mark(f"st_pupd{g}_{c}_{fb}")
                dv.dmark(f"st_vdone{g}")

            def pe_sy():
                for g in range(GQ):
                    for fb in range(8):
                        ft = FT_HID0 + g * 8 + fb
                        sy.waitm(f"silu{ft}")
                        sy.dma(hidb[(g * 8 + fb) * 128:(g * 8 + fb + 1) * 128, :],
                               hidg[:, (g % 2) * 8192 + fb * 1024:(g % 2) * 8192 + (fb + 1) * 1024])
                    sy.mark(f"hspill{g}")
                    sy.waitm(f"silu{FT_B0 + g}")
                    sy.dma(bcb[(2 * g) * 128:(2 * g + 1) * 128, :],
                           BgS[:, (g % 2) * 1024:(g % 2) * 1024 + 1024])
                    sy.mark(f"bspill{g}")
                    sy.waitm(f"silu{FT_C0 + g}")
                    sy.dma(bcb[(2 * g + 1) * 128:(2 * g + 2) * 128, :],
                           CgS[:, (g % 2) * 1024:(g % 2) * 1024 + 1024])
                    sy.mark(f"cspill{g}")
                    if stage >= 5:
                        for c in range(1, NCH):
                            sy.waitm(f"st_pupd{g}_{c-1}_7")
                            sy.dma(pclb[(g * 8 + c) * 128:(g * 8 + c + 1) * 128, :],
                                   Pg[:, (c % 2) * 1024:(c % 2) * 1024 + 1024])
                            sy.mark(f"psnap{g}_{c}")
                        sy.waitm(f"st_pupd{g}_{NCH-1}_7")
                        sy.dma(ccin[:, g * 1024:(g + 1) * 1024], Pg[:, 0:1024])
                        sy.mark(f"ccin{g}")
                        sy.waitm(f"st_vdone{g}")
                        sy.dma(hcb[g * 128:(g + 1) * 128, :], hcg[:])
                        sy.mark(f"hcspill{g}")
                        sy.mark(f"st_done{g}")

            if stage >= 4:
                _run_phase(ctx, nc, [
                    ("gpsimd", gp, pe_gp),
                    ("tensor", pe, pe_pe),
                    ("scalar", ac, pe_ac),
                    ("vector", dv, pe_dv),
                    ("sync", sy, pe_sy),
                ])

            sE1.close()
            sE2.close()

            if stage >= 6:
                with contextlib.ExitStack() as sF:
                    SBF = lambda n, s, t: sF.enter_context(nc.sbuf_tensor(n, s, t))
                    S_acc = SBF("S_acc", [128, 8192], F32)
                    gbuf = SBF("gbuf", [128, 2 * 1024], BF16)
                    gmt_sb = SBF("gmt_sb", [8, 128], F32)
                    mcol_sb = SBF("mcol_sb", [8, 1], F32)
                    atot_sb = SBF("atot_sb", [8, 128], F32)
                    wsb = SBF("wsb", [8, 128], F32)
                    wTsb = SBF("wTsb", [128, 8], F32)
                    wjb = SBF("wjb", [128, 128], F32)
                    wjT = SBF("wjT", [128, 128], F32)
                    sjl = SBF("sjl", [128, 2 * 4096], F32)
                    ps_g1 = sF.enter_context(nc.psum_tensor("ps_g1", [128, 512], F32))
                    ps_g2 = sF.enter_context(nc.psum_tensor("ps_g2", [128, 512], F32))

                    def pf_gp():
                        if not NO_SIN:
                            gp.waitm("dt_spill")
                            for g in range(GQ):
                                gp.waitm(f"ccin{g}")
                            gp.wait_queue_idle(gp)
                            gp.op("collective_compute", "AllGather", ALU.bypass,
                                  replica_groups=RG8, ins=[ccin.opt()], outs=[ccout.opt()],
                                  inc=1, semname="CK")
                            gp.op("collective_compute", "AllGather", ALU.bypass,
                                  replica_groups=RG8, ins=[ccin2.opt()], outs=[ccout2.opt()],
                                  inc=1, semname="CK")
                            gp.wait_count("CK", 2)
                        for j in range(64):
                            slot = j % 2
                            if j == 0:
                                gp.waitm(f"mm{p1a_tiles[78]}")
                            elif j == 1:
                                gp.waitm(f"mm{p1a_tiles[79]}")
                            else:
                                gp.waitm(f"gmm{j-2}")
                            gp.dma(wbuf[:, slot * DM:(slot + 1) * DM], win[j * 128:(j + 1) * 128, :])
                            gp.mark(f"gw{j}")

                    def pf_pe():
                        for j in range(64):
                            slot = j % 2
                            banks = [ps_in[0], ps_in[1]] if j % 2 == 0 else [ps_in[3], ps_in[4]]
                            pe.waitm(f"gw{j}")
                            if j == 0:
                                pe.waitm(f"ev{p1a_tiles[79]}")
                            elif j == 1:
                                pe.waitm(f"ev{p1a_tiles[78]}")
                            else:
                                pe.waitm(f"gev{j-2}")
                            for dt in range(NDT):
                                for gi, (c0, n) in enumerate(G_GATE):
                                    pe.op("matmul", banks[gi][:, 0:n],
                                          wbuf[:, slot * DM + dt * 128: slot * DM + (dt + 1) * 128],
                                          hT[:, dt * TW + c0: dt * TW + c0 + n],
                                          start=(dt == 0), stop=(dt == NDT - 1),
                                          inc=1 if gi == 1 else 0)
                            pe.mark(f"gmm{j}")
                        if NO_SIN:
                            return
                        pe.waitm("f_w")
                        pe.op("matmul", ps_g1[:, 0:128], gmt_sb[:], atot_sb[:],
                              start=True, stop=True, inc=1)
                        pe.mark("f_cmm")
                        pe.waitm("f_wm")
                        pe.op("transpose", ps_g2[:, 0:8], wsb[:], idf[0:8, 0:8], inc=1)
                        pe.mark("f_wT")
                        for j in range(8):
                            pe.waitm(f"f_wjb{j}")
                            if j >= 1:
                                pe.waitm(f"f_wjev{j-1}")
                            pe.op("transpose", ps_g2[:, 128:256], wjT[:], idf[:], inc=1)
                            pe.mark(f"f_wjt{j}")

                    def pf_ac():
                        ac.waitm(f"st_vdone{GQ-1}")
                        for j in range(64):
                            slot = j % 2
                            banks = [ps_in[0], ps_in[1]] if j % 2 == 0 else [ps_in[3], ps_in[4]]
                            ac.waitm(f"gmm{j}")
                            if j >= 2:
                                ac.waitm(f"gspill{j-2}")
                            for gi in range(2):
                                ac.op("copy", gbuf[:, slot * 1024 + gi * 512: slot * 1024 + (gi + 1) * 512],
                                      banks[gi][:, 0:512])
                            ac.dmark(f"gev{j}")
                        if NO_SIN:
                            return
                        ac.waitm("f_cmm")
                        ac.op("activation", wsb[:], ps_g1[0:8, 0:128], AF.Exp)
                        ac.mark("f_exp")
                        ac.waitm("f_wT")
                        ac.op("copy", wTsb[:], ps_g2[:, 0:8])
                        ac.mark("f_wTsb")
                        for j in range(8):
                            ac.waitm("f_wTsb")
                            if j >= 1:
                                ac.waitm(f"f_wjt{j-1}")
                            ac.op("activation", wjT[:], acs_tok[:, 0:128],
                                  AF.Identity, scale=0.0, bias=wTsb[:, j:j + 1])
                            ac.mark(f"f_wjb{j}")

                    def pf_dv():
                        if NO_SIN:
                            return
                        dv.waitm("f_exp")
                        dv.op("tensor_scalar_mul", wsb[:], wsb[:], mcol_sb[:])
                        dv.mark("f_wm")
                        for j in range(8):
                            dv.waitm(f"f_wjt{j}")
                            dv.op("tensor_copy", wjb[:], ps_g2[:, 128:256])
                            dv.mark(f"f_wjev{j}")
                            for half in range(2):
                                dv.waitm(f"f_sl{j}_{half}")
                                wrep = wjb[:, half * 64:(half + 1) * 64].unsqueeze(2).broadcast_to([128, 64, 64])
                                s3 = sjl[:, (half % 2) * 4096:(half % 2) * 4096 + 4096].rearrange(
                                    "n (h p) -> n h p", h=64)
                                d3 = S_acc[:, half * 4096:(half + 1) * 4096].rearrange(
                                    "n (h p) -> n h p", h=64)
                                if j == 0:
                                    dv.op("tensor_tensor", d3, s3, wrep, ALU.mult)
                                else:
                                    dv.op("tensor_tensor", s3, s3, wrep, ALU.mult)
                                    dv.op("tensor_add", d3, d3, s3)
                                dv.mark(f"f_acc{j}_{half}")
                        dv.waitm("f_acc7_1")
                        dv.op("tensor_copy", S_in_bf[:], S_acc[:])
                        dv.dmark("f_sin")

                    def pf_sy():
                        sy.waitm(f"st_vdone{GQ-1}")
                        sy.waitm(f"st_done{GQ-1}")
                        sy.dma(gmt_sb[:], gmt)
                        sy.dma(mcol_sb[:], mcolA)
                        sy.mark("f_consts")
                        if NO_SIN:
                            for j in range(64):
                                sy.waitm(f"gev{j}")
                                sy.dma(gateb[j * 128:(j + 1) * 128, :],
                                       gbuf[:, (j % 2) * 1024:(j % 2) * 1024 + 1024])
                                sy.mark(f"gspill{j}")
                            return
                        for j in range(64):
                            sy.waitm(f"gev{j}")
                            sy.dma(gateb[j * 128:(j + 1) * 128, :],
                                   gbuf[:, (j % 2) * 1024:(j % 2) * 1024 + 1024])
                            sy.mark(f"gspill{j}")
                        sy.wait_count("CK", 2)
                        sy.dma(atot_sb[:], ccout2)
                        sy.mark("f_w")
                        for j in range(8):
                            for half in range(2):
                                if j * 2 + half >= 2:
                                    pj, ph = (j * 2 + half - 2) // 2, (j * 2 + half - 2) % 2
                                    sy.waitm(f"f_acc{pj}_{ph}")
                                sy.dma(sjl[:, (half % 2) * 4096:(half % 2) * 4096 + 4096],
                                       ccout[j * 128:(j + 1) * 128, half * 4096:(half + 1) * 4096])
                                sy.mark(f"f_sl{j}_{half}")
                        sy.waitm("f_sin")
                        sy.dma(sind[:, 4096:8192], S_acc[:, 0:4096])

                    _run_phase(ctx, nc, [
                        ("sync", sy, pf_sy),
                        ("gpsimd", gp, pf_gp),
                        ("tensor", pe, pf_pe),
                        ("scalar", ac, pf_ac),
                        ("vector", dv, pf_dv),
                    ])

            if stage >= 6:
                sDT2.close()
                sH.close()
        if stage >= 7:
            with contextlib.ExitStack() as sG:
                SBG = lambda n, s, t: sG.enter_context(nc.sbuf_tensor(n, s, t))
                PSG = lambda n, s, t: sG.enter_context(nc.psum_tensor(n, s, t))
                hid2 = SBG("hid2", [128, 2 * 8192], BF16)
                hc2 = SBG("hc2", [128, 2 * 8192], BF16)
                gate2 = SBG("gate2", [128, 2 * 8192], BF16)
                Bg2 = SBG("Bg2", [128, 2 * 1024], BF16)
                Cg2 = SBG("Cg2", [128, 2 * 1024], BF16)
                pl2 = SBG("pl2", [128, 2 * 1024], BF16)
                csfl = SBG("csfl", [1, 2 * 2048], F32)
                eo_all = SBG("eo_all", [128, 8 * 128], F32)
                eoT_t = SBG("eoT_t", [128, 128], F32)
                LtB = SBG("LtB", [128, 2 * 1024], BF16)
                EAB = SBG("EAB", [128, 2 * 1024], BF16)
                MthB = SBG("MthB", [128, 2 * 1024], BF16)
                ChsB = SBG("ChsB", [128, 2 * 1024], BF16)
                t1f = SBG("t1f", [128, 1024], F32)
                GtB = SBG("GtB", [128, 128], BF16)
                Pfull = SBG("Pfull", [128, 2 * 1024], BF16)
                y2 = SBG("y2", [128, 8192], BF16)
                sq2 = SBG("sq2", [128, 1024], BF16)
                rstd_bf = SBG("rstd_bf", [128, 1024], BF16)

                psR = PSG("psR", [128, 1024], F32)
                psG_b = PSG("psG_b", [128, 512], F32)
                psY = [PSG(f"psY{i}", [128, 512], F32)[:, 0:128] for i in range(2)]
                psS = PSG("psS", [128, 1024], F32)

                def head_slices(c, g, fb):
                    half = fb // 4
                    lh0 = 2 * fb - half * 8
                    return half, lh0

                def pg_sy():
                    sy.waitm("f_sin")
                    sy.waitm("gspill63")

                    def rl(g):
                        slot = g % 2
                        if g >= 2:
                            sy.waitm(f"g_pe_done{g-2}")
                            sy.waitm(f"g_dv_done{g-2}")
                        for fb in range(8):
                            sy.dma(hid2[:, slot * 8192 + fb * 1024: slot * 8192 + (fb + 1) * 1024],
                                   hidb[(g * 8 + fb) * 128:(g * 8 + fb + 1) * 128, :])
                        sy.mark(f"rl_hid{g}")
                        sy.dma(hc2[:, slot * 8192:(slot + 1) * 8192], hcb[g * 128:(g + 1) * 128, :])
                        sy.mark(f"rl_hc{g}")
                        for fb in range(8):
                            sy.dma(gate2[:, slot * 8192 + fb * 1024: slot * 8192 + (fb + 1) * 1024],
                                   gateb[(g * 8 + fb) * 128:(g * 8 + fb + 1) * 128, :])
                        sy.mark(f"rl_gate{g}")
                        sy.dma(Bg2[:, slot * 1024:(slot + 1) * 1024], bcb[2 * g * 128:(2 * g + 1) * 128, :])
                        sy.dma(Cg2[:, slot * 1024:(slot + 1) * 1024], bcb[(2 * g + 1) * 128:(2 * g + 2) * 128, :])
                        sy.mark(f"rl_bc{g}")

                    NG = G_LIM[0] if G_LIM else GQ
                    rl(0)
                    for g in range(NG):
                        for c in range(G_LIM[1] if G_LIM else NCH):
                            if g * 8 + c >= 2:
                                pc_, cc_ = (g * 8 + c - 2) // 8, (g * 8 + c - 2) % 8
                                sy.waitm(f"rowb{pc_}_{cc_}_1")
                            sy.dma(csfl[0:1, (c % 2) * 2048:(c % 2) * 2048 + 2048].rearrange(
                                       "one (h i) -> one h i", h=16),
                                   acsb[16 * g:16 * (g + 1), 128 * c:128 * (c + 1)].unsqueeze(0))
                            sy.mark(f"csfl{g}_{c}")
                            if c == 2 and g + 1 < NG:
                                rl(g + 1)

                def pg_gp():
                    gp.waitm("f_sin")
                    gp.waitm("gspill63")
                    for g in range(G_LIM[0] if G_LIM else GQ):
                        for c in range(1, G_LIM[1] if G_LIM else NCH):
                            if c >= 3:
                                gp.waitm(f"pf{g}_{c-2}")
                            elif g >= 1:
                                gp.waitm(f"pf{g-1}_{7 if c == 1 else 6}")
                            gp.dma(pl2[:, (c % 2) * 1024:(c % 2) * 1024 + 1024],
                                   pclb[(g * 8 + c) * 128:(g * 8 + c + 1) * 128, :])
                            gp.mark(f"plrl{g}_{c}")
                        gp.waitm(f"g_dv_done{g}")
                        for fb in range(8):
                            gp.dma(ybuf[(g * 8 + fb) * 128:(g * 8 + fb + 1) * 128, :],
                                   y2[:, fb * 1024:(fb + 1) * 1024])
                        gp.mark(f"yspill{g}")

                def pg_ac():
                    ac.waitm("f_sin")
                    for c in range(NCH):
                        ac.waitm("offs")
                        if c >= 1:
                            ac.waitm(f"eot{c-1}")
                        ac.op("activation", eoT_t[:], acs_tok[:, 0:128], AF.Exp,
                              scale=0.0, bias=offs[:, c:c + 1])
                        ac.mark(f"eoT{c}")
                        ac.waitm(f"eot{c}")
                        ac.op("copy", eo_all[:, c * 128:(c + 1) * 128], psG_b[:, 0:128])
                        ac.mark(f"eo{c}")
                    for g in range(G_LIM[0] if G_LIM else GQ):
                        slot = g % 2
                        for c in range(G_LIM[1] if G_LIM else NCH):
                            for half in range(2):
                                hp = ((g * 8 + c) * 2 + half) % 2
                                ac.waitm(f"t2_{g}_{c}_{half}")
                                ph = (g * 8 + c) * 2 + half - 2
                                if ph >= 0:
                                    pg_, pc_, phf = ph // 16, (ph // 2) % 8, ph % 2
                                    ac.waitm(f"mth{pg_}_{pc_}_{phf}")
                                    ac.waitm(f"chs{pg_}_{pc_}_{phf}")
                                ac.op("activation", LtB[:, hp * 1024:(hp + 1) * 1024],
                                      t1f[:], AF.Exp)
                                ac.mark(f"lt{g}_{c}_{half}")
                                ac.waitm(f"rowb{g}_{c}_{half}")
                                ac.op("activation", EAB[:, hp * 1024:(hp + 1) * 1024],
                                      psR[:], AF.Exp)
                                ac.mark(f"ea{g}_{c}_{half}")
                        for fb in range(8):
                            ac.waitm(f"rl_gate{g}")
                            if g >= 1:
                                ac.waitm(f"g_dv_done{g-1}")
                            ac.op("activation", gate2[:, slot * 8192 + fb * 1024: slot * 8192 + (fb + 1) * 1024],
                                  gate2[:, slot * 8192 + fb * 1024: slot * 8192 + (fb + 1) * 1024],
                                  AF.Silu)
                            ac.mark(f"gsl{g}_{fb}")
                        if not NO_GNORM:
                            for fb in range(8):
                                ac.waitm(f"yg{g}_{fb}")
                                if fb >= 1:
                                    ac.waitm(f"sqmm{g}_{fb-1}")
                                ac.op("activation", sq2[:], y2[:, fb * 1024:(fb + 1) * 1024], AF.Square)
                                ac.mark(f"sq{g}_{fb}")
                            ac.waitm(f"ssq{g}")
                            ac.op("activation", rstd_bf[:], psS[:], AF.Ln, scale=1.0 / 1024.0, bias=epscol[:])
                            ac.op("activation", rstd_bf[:], rstd_bf[:], AF.Exp, scale=-0.5)
                            ac.mark(f"rstd{g}")

                def pg_pe():
                    pe.waitm("f_sin")
                    for c in range(NCH):
                        pe.waitm(f"eoT{c}")
                        if c >= 1:
                            pe.waitm(f"eo{c-1}")
                        pe.op("transpose", psG_b[:, 0:128], eoT_t[:], idf[:], inc=1)
                        pe.mark(f"eot{c}")
                    for g in range(G_LIM[0] if G_LIM else GQ):
                        slot = g % 2
                        for c in range(G_LIM[1] if G_LIM else NCH):
                            pe.waitm(f"rl_bc{g}")
                            if g == 0 and c == 0:
                                pe.waitm(f"eo{NCH-1}")
                            if g * 8 + c >= 1:
                                pidx = g * 8 + c - 1
                                pe.waitm(f"gtev{pidx // 8}_{pidx % 8}")
                            pe.op("matmul", psG_b[:, 0:128],
                                  Bg2[:, slot * 1024 + c * 128: slot * 1024 + (c + 1) * 128],
                                  Cg2[:, slot * 1024 + c * 128: slot * 1024 + (c + 1) * 128],
                                  start=True, stop=True, inc=1)
                            pe.mark(f"gmmG{g}_{c}")
                            for half in range(2):
                                pe.waitm(f"csfl{g}_{c}")
                                if (g * 8 + c, half) != (0, 0):
                                    ph = (g * 8 + c) * 2 + half - 1
                                    pg_, pc_, phf = ph // 16, (ph // 2) % 8, ph % 2
                                    pe.waitm(f"t1_{pg_}_{pc_}_{phf}")
                                    pe.waitm(f"ea{pg_}_{pc_}_{phf}")
                                for piece in range(2):
                                    pe.op("matmul", psR[:, piece * 512:(piece + 1) * 512], ones_fr[:],
                                          csfl[0:1, (c % 2) * 2048 + half * 1024 + piece * 512:
                                               (c % 2) * 2048 + half * 1024 + (piece + 1) * 512],
                                          start=True, stop=True, inc=(1 if piece == 1 else 0))
                                pe.mark(f"rowb{g}_{c}_{half}")
                                hp = ((g * 8 + c) * 2 + half) % 2
                                cp = (g * 8 + c) % 2
                                for fbl in range(4):
                                    fb = half * 4 + fbl
                                    bank = psY[fb % 2]
                                    _, lh0 = head_slices(c, g, fb)
                                    pe.waitm(f"mth{g}_{c}_{half}")
                                    pe.waitm(f"chs{g}_{c}_{half}")
                                    pe.waitm(f"pf{g}_{c}")
                                    pe.waitm(f"rl_hc{g}")
                                    yidx = (g * 8 + c) * 8 + fb - 2
                                    if yidx >= 0:
                                        yg_, yc_, yf_ = yidx // 64, (yidx // 8) % 8, yidx % 8
                                        pe.waitm(f"ya{yg_}_{yc_}_{yf_}")
                                    for hh in range(2):
                                        lh = lh0 + hh
                                        pe.op("matmul", bank[hh * 64:(hh + 1) * 64, :],
                                              hc2[:, slot * 8192 + c * 1024 + fb * 128 + hh * 64:
                                                  slot * 8192 + c * 1024 + fb * 128 + (hh + 1) * 64],
                                              MthB[:, hp * 1024 + lh * 128: hp * 1024 + (lh + 1) * 128],
                                              start=True, stop=False,
                                              tile_position=(0, 64 * hh), inc=0)
                                        pe.op("matmul", bank[hh * 64:(hh + 1) * 64, :],
                                              Pfull[:, cp * 1024 + (2 * fb + hh) * 64:
                                                    cp * 1024 + (2 * fb + hh + 1) * 64],
                                              ChsB[:, hp * 1024 + lh * 128: hp * 1024 + (lh + 1) * 128],
                                              start=False, stop=True,
                                              tile_position=(0, 64 * hh),
                                              inc=(1 if hh == 1 else 0))
                                    pe.mark(f"ymm{g}_{c}_{fb}")
                        if not NO_GNORM:
                            for fb in range(8):
                                pe.waitm(f"sq{g}_{fb}")
                                if fb == 0:
                                    pe.waitm(f"rstd{g-1}" if g >= 1 else "c_ones")
                                for piece in range(2):
                                    pe.op("matmul", psS[:, piece * 512:(piece + 1) * 512], ones_bb[:],
                                          sq2[:, piece * 512:(piece + 1) * 512],
                                          start=(fb == 0), stop=(fb == 7),
                                          inc=(1 if piece == 1 else 0))
                                pe.mark(f"sqmm{g}_{fb}")
                            pe.mark(f"ssq{g}")
                        pe.mark(f"g_pe_done{g}")

                def pg_dv():
                    if G_LIM:
                        dv.op("memset", y2[:], 0.0)
                    dv.mark("g_zero")
                    for g in range(G_LIM[0] if G_LIM else GQ):
                        slot = g % 2
                        for c in range(G_LIM[1] if G_LIM else NCH):
                            srep = S_in_bf[:, g * 1024:(g + 1) * 1024].rearrange("n (h p) -> n h p", h=16)
                            eorep = eo_all[:, c * 128 + g * 16: c * 128 + (g + 1) * 16]
                            eorep = eorep.unsqueeze(2).broadcast_to([128, 16, 64])
                            dv.waitm(f"eo{c}")
                            dv.waitm("f_sin")
                            cp = (g * 8 + c) % 2
                            pidx = g * 8 + c - 2
                            if pidx >= 0:
                                dv.waitm(f"pf_used{pidx // 8}_{pidx % 8}")
                            Pfs = Pfull[:, cp * 1024:(cp + 1) * 1024]
                            dv.op("tensor_tensor", Pfs.rearrange("n (h p) -> n h p", h=16),
                                  srep, eorep, ALU.mult)
                            if c != 0:
                                dv.waitm(f"plrl{g}_{c}")
                                dv.op("tensor_add", Pfs, Pfs, pl2[:, (c % 2) * 1024:(c % 2) * 1024 + 1024])
                            dv.mark(f"pf{g}_{c}")
                            dv.waitm(f"gmmG{g}_{c}")
                            dv.op("tensor_tensor", GtB[:], psG_b[:, 0:128], trimask[:], ALU.mult)
                            dv.mark(f"gtev{g}_{c}")
                            for half in range(2):
                                dv.waitm(f"rowb{g}_{c}_{half}")
                                if (g * 8 + c, half) != (0, 0):
                                    ph = (g * 8 + c) * 2 + half - 1
                                    pg_, pc_, phf = ph // 16, (ph // 2) % 8, ph % 2
                                    dv.waitm(f"lt{pg_}_{pc_}_{phf}")
                                colb = acs_tok[:, c * 128 + g * 16 + half * 8: c * 128 + g * 16 + half * 8 + 8]
                                colb = colb.unsqueeze(2).broadcast_to([128, 8, 128])
                                dv.op("tensor_tensor", t1f[:].rearrange("p (h i) -> p h i", h=8),
                                      psR[:].rearrange("p (h i) -> p h i", h=8), colb, ALU.subtract)
                                dv.mark(f"t1_{g}_{c}_{half}")
                                dv.op("tensor_scalar_min", t1f[:], t1f[:], 0.0)
                                dv.mark(f"t2_{g}_{c}_{half}")
                                hp = ((g * 8 + c) * 2 + half) % 2
                                grep = GtB[:].unsqueeze(1).broadcast_to([128, 8, 128])
                                dv.waitm(f"lt{g}_{c}_{half}")
                                ph = (g * 8 + c) * 2 + half - 2
                                if ph >= 0:
                                    pg_, pc_, phf = ph // 16, (ph // 2) % 8, ph % 2
                                    dv.waitm(f"ymm{pg_}_{pc_}_{phf * 4 + 3}")
                                dv.op("tensor_tensor",
                                      MthB[:, hp * 1024:(hp + 1) * 1024].rearrange("p (h i) -> p h i", h=8),
                                      LtB[:, hp * 1024:(hp + 1) * 1024].rearrange("p (h i) -> p h i", h=8),
                                      grep, ALU.mult)
                                dv.mark(f"mth{g}_{c}_{half}")
                                crep = Cg2[:, slot * 1024 + c * 128: slot * 1024 + (c + 1) * 128]
                                crep = crep.unsqueeze(1).broadcast_to([128, 8, 128])
                                dv.waitm(f"ea{g}_{c}_{half}")
                                dv.op("tensor_tensor",
                                      ChsB[:, hp * 1024:(hp + 1) * 1024].rearrange("p (h i) -> p h i", h=8),
                                      EAB[:, hp * 1024:(hp + 1) * 1024].rearrange("p (h i) -> p h i", h=8),
                                      crep, ALU.mult)
                                dv.mark(f"chs{g}_{c}_{half}")
                                for fbl in range(4):
                                    fb = half * 4 + fbl
                                    dv.waitm(f"ymm{g}_{c}_{fb}")
                                    dv.waitm(f"rl_hid{g}")
                                    if g >= 1 and c == 0 and fb == 0:
                                        dv.waitm(f"yspill{g-1}")
                                    dv.op("scalar_tensor_tensor",
                                          y2[:, fb * 1024 + c * 128: fb * 1024 + (c + 1) * 128],
                                          hid2[:, slot * 8192 + fb * 1024 + c * 128:
                                               slot * 8192 + fb * 1024 + (c + 1) * 128],
                                          dpcol[:, g * 8 + fb:g * 8 + fb + 1],
                                          psY[fb % 2], ALU.mult, ALU.add)
                                    dv.mark(f"ya{g}_{c}_{fb}")
                            dv.waitm(f"ymm{g}_{c}_7")
                            dv.mark(f"pf_used{g}_{c}")
                        for fb in range(8):
                            dv.waitm(f"gsl{g}_{fb}")
                            if not NO_GNORM:
                                dv.op("tensor_mul", y2[:, fb * 1024:(fb + 1) * 1024],
                                      y2[:, fb * 1024:(fb + 1) * 1024],
                                      gate2[:, slot * 8192 + fb * 1024: slot * 8192 + (fb + 1) * 1024])
                            else:
                                dv.op("nop", inc=1)
                            dv.mark(f"yg{g}_{fb}")
                        for fb in range(8):
                            if not NO_GNORM:
                                dv.waitm(f"rstd{g}")
                                dv.op("scalar_tensor_tensor", y2[:, fb * 1024:(fb + 1) * 1024],
                                      y2[:, fb * 1024:(fb + 1) * 1024],
                                      gncol[:, g * 8 + fb:g * 8 + fb + 1],
                                      rstd_bf[:], ALU.mult, ALU.mult)
                            else:
                                dv.op("nop", inc=1)
                        dv.op("drain", inc=1)
                        dv.mark(f"g_dv_done{g}")

                _run_phase(ctx, nc, [
                    ("sync", sy, pg_sy),
                    ("gpsimd", gp, pg_gp),
                    ("scalar", ac, pg_ac),
                    ("tensor", pe, pg_pe),
                    ("vector", dv, pg_dv),
                ])
        sSin.close()

        sDT.close()
        if stage >= 8:
            with contextlib.ExitStack() as sP2:
                SB2 = lambda n, s, t: sP2.enter_context(nc.sbuf_tensor(n, s, t))
                ySB = SB2("ySB", [128, 64 * 1024], BF16)
                wo2 = SB2("wo2", [128, 2 * 8192], BF16)
                xt2 = SB2("xt2", [128, 2 * 1024], F32)
                xaev = SB2("xaev", [128, 2 * 1024], F32)
                onepb = SB2("onepb", [128, 1024], F32)
                mrb = SB2("mrb", [128, 1024], F32)
                r1p = SB2("r1p", [1, TW], F32)
                r2p = SB2("r2p", [1, TW], F32)
                psO = [sP2.enter_context(nc.psum_tensor(f"psO{i}", [128, 1024], F32)) for i in range(2)]
                psB2 = sP2.enter_context(nc.psum_tensor("psB2", [128, 1024], F32))

                def ph_sy():
                    sy.waitm(f"yspill{GQ-1}")
                    sy.waitm(f"g_pe_done{GQ-1}")
                    sy.waitm(f"g_dv_done{GQ-1}")
                    for ft in range(64):
                        sy.dma(ySB[:, ft * 1024:(ft + 1) * 1024],
                               ybuf[ft * 128:(ft + 1) * 128, :])
                    sy.mark("rl_y")
                    for dt in range(NDT):
                        if dt >= 2:
                            sy.waitm(f"xaev{dt-2}")
                        sy.dma(xt2[:, (dt % 2) * 1024:(dt % 2) * 1024 + 1024],
                               xt[:, dt * TW + 3: dt * TW + 3 + 1024])
                        sy.mark(f"xrl{dt}")
                        sy.waitm(f"xadone{dt}")
                        sy.dma(xab[dt * 128:(dt + 1) * 128, :],
                               xaev[:, (dt % 2) * 1024:(dt % 2) * 1024 + 1024])
                        sy.mark(f"xaev{dt}")

                def ph_gp():
                    gp.waitm(f"yspill{GQ-1}")
                    gp.waitm(f"g_pe_done{GQ-1}")
                    gp.waitm(f"g_dv_done{GQ-1}")
                    for dt in range(NDT):
                        if dt >= 2:
                            gp.waitm(f"omm{dt-2}")
                        gp.dma(wo2[:, (dt % 2) * 8192:(dt % 2) * 8192 + 8192],
                               wout[dt * 128:(dt + 1) * 128, :])
                        gp.mark(f"wo{dt}")

                def ph_dv():
                    dv.waitm(f"g_pe_done{GQ-1}")
                    dv.waitm(f"rstd{GQ-1}")
                    dv.op("tensor_scalar_add", r1p[0:1, :], rsrow[0:1, :], 1.0)
                    dv.op("tensor_mul", r2p[0:1, :], mrow[0:1, :], rsrow[0:1, :])
                    dv.dmark("p2rows")
                    dv.waitm("p2bc")
                    dv.op("tensor_copy", onepb[:], psB2[:, 0:1024])
                    dv.mark("p2bc1")
                    dv.waitm("p2bc2m")
                    dv.op("tensor_copy", mrb[:], psB2[:, 0:1024])
                    dv.mark("p2bc2")
                    for dt in range(NDT):
                        dv.waitm(f"omm{dt}")
                        dv.waitm(f"xrl{dt}")
                        if dt >= 2:
                            dv.waitm(f"xaev{dt-2}")
                        sl = xaev[:, (dt % 2) * 1024:(dt % 2) * 1024 + 1024]
                        xsl = xt2[:, (dt % 2) * 1024:(dt % 2) * 1024 + 1024]
                        dv.op("tensor_mul", sl, xsl, onepb[:])
                        dv.op("tensor_sub", sl, sl, mrb[:])
                        dv.op("tensor_add", sl, sl, psO[dt % 2][:])
                        dv.op("drain", inc=1)
                        dv.mark(f"xadone{dt}")

                def ph_pe():
                    pe.waitm(f"g_dv_done{GQ-1}")
                    pe.waitm(f"rstd{GQ-1}")
                    pe.waitm("p2rows")
                    for piece in range(2):
                        pe.op("matmul", psB2[:, piece * 512:(piece + 1) * 512], ones_fr[:],
                              r1p[0:1, 3 + piece * 512: 3 + (piece + 1) * 512],
                              start=True, stop=True, inc=(1 if piece == 1 else 0))
                    pe.mark("p2bc")
                    pe.waitm("p2bc1")
                    for piece in range(2):
                        pe.op("matmul", psB2[:, piece * 512:(piece + 1) * 512], ones_fr[:],
                              r2p[0:1, 3 + piece * 512: 3 + (piece + 1) * 512],
                              start=True, stop=True, inc=(1 if piece == 1 else 0))
                    pe.mark("p2bc2m")
                    for dt in range(NDT):
                        pe.waitm(f"wo{dt}")
                        if dt == 0:
                            pe.waitm("rl_y")
                        if dt >= 2:
                            pe.waitm(f"xadone{dt-2}")
                        for f in range(64):
                            for tg in range(2):
                                pe.op("matmul", psO[dt % 2][:, tg * 512:(tg + 1) * 512],
                                      wo2[:, (dt % 2) * 8192 + f * 128:(dt % 2) * 8192 + (f + 1) * 128],
                                      ySB[:, f * 1024 + tg * 512: f * 1024 + (tg + 1) * 512],
                                      start=(f == 0), stop=(f == 63),
                                      inc=(1 if (f == 63 and tg == 1) else 0))
                        pe.mark(f"omm{dt}")

                _run_phase(ctx, nc, [
                    ("sync", sy, ph_sy), ("gpsimd", gp, ph_gp),
                    ("vector", dv, ph_dv), ("tensor", pe, ph_pe),
                ])

        if stage >= 9:
            with contextlib.ExitStack() as sP3:
                SB3 = lambda n, s, t: sP3.enter_context(nc.sbuf_tensor(n, s, t))
                xfT = SB3("xfT", [128, 32 * 1024], BF16)
                bw3 = SB3("bw3", [128, 2 * 4096], BF16)
                bp3 = SB3("bp3", [128, 2 * 1024], BF16)
                rhp = SB3("rhp", [128, 8 * 1024], BF16)
                rout3 = SB3("rout3", [128, 512], BF16)
                at3 = SB3("at3", [128, 2048], BF16)
                sq3 = SB3("sq3", [128, 1024], BF16)
                mb3 = SB3("mb3", [128, 1024], BF16)
                rs3 = SB3("rs3", [128, 1024], BF16)
                v3a = SB3("v3a", [128, 1024], F32)
                v3b = SB3("v3b", [128, 1024], F32)
                ce3 = SB3("ce3", [128, 16], F32)
                pr3 = SB3("pr3", [128, 16], F32)
                sc3 = SB3("sc3", [128, 8], F32)
                rhb = SB3("rhb", [128, 64], BF16)
                rhe = SB3("rhe", [128, 64], BF16)
                xa3 = SB3("xa3", [128, 2 * 1024], F32)
                ob3 = SB3("ob3", [128, 2 * 1024], F32)
                psOa = sP3.enter_context(nc.psum_tensor("psOa", [128, 1024], F32))
                psOb = sP3.enter_context(nc.psum_tensor("psOb", [128, 1024], F32))
                psRt = sP3.enter_context(nc.psum_tensor("psRt", [128, 512], F32))
                psR2x = sP3.enter_context(nc.psum_tensor("psR2x", [128, 512], F32))
                psT3 = sP3.enter_context(nc.psum_tensor("psT3", [128, 1024], BF16))
                psO3 = [psOa, psOb]

                def pi_gp():
                    gp.waitm(f"xaev{NDT-1}")
                    gp.waitm(f"omm{NDT-1}")
                    gp.waitm(f"xadone{NDT-1}")
                    gp.dma(rout3[:], wrout)
                    gp.dma(at3[:], wat)
                    for dt in range(NDT):
                        gp.dma(xfT[:, dt * 1024:(dt + 1) * 1024],
                               xab[dt * 128:(dt + 1) * 128, :])
                    gp.mark("i_xf")
                    for dt in range(NDT):
                        if dt >= 2:
                            gp.waitm(f"fmm{dt-2}")
                        gp.dma(bw3[:, (dt % 2) * 4096:(dt % 2) * 4096 + 4096],
                               wbase[dt * 128:(dt + 1) * 128, :])
                        gp.dma(bp3[:, (dt % 2) * 1024:(dt % 2) * 1024 + 1024],
                               wbp[dt * 128:(dt + 1) * 128, :])
                        gp.mark(f"bw{dt}")

                def pi_pe():
                    pe.waitm(f"xadone{NDT-1}")
                    pe.waitm(f"xaev{NDT-1}")
                    pe.waitm("i_xf")
                    for dt in range(NDT):
                        for tg in range(2):
                            pe.op("matmul", psOa[:, tg * 512:(tg + 1) * 512], ones_bb[:],
                                  xfT[:, dt * 1024 + tg * 512: dt * 1024 + (tg + 1) * 512],
                                  start=(dt == 0), stop=(dt == NDT - 1), inc=1)
                    pe.mark("i_sum")
                    for dt in range(NDT):
                        pe.waitm(f"i_sq{dt}")
                        for tg in range(2):
                            pe.op("matmul", psOb[:, tg * 512:(tg + 1) * 512], ones_bb[:],
                                  sq3[:, tg * 512:(tg + 1) * 512],
                                  start=(dt == 0), stop=(dt == NDT - 1), inc=1)
                        pe.mark(f"i_qmm{dt}")
                    pe.mark("i_sumsq")
                    for tt in range(8):
                        pe.waitm(f"i_xfok{NDT-1}")
                        if tt >= 1:
                            pe.waitm(f"i_rh_ev{tt-1}")
                            pe.waitm(f"i_sm{tt-1}")
                        for dt in range(NDT):
                            pe.op("matmul", psRt[:, 0:16],
                                  xfT[:, dt * 1024 + tt * 128: dt * 1024 + tt * 128 + 128],
                                  rout3[:, dt * 16:(dt + 1) * 16],
                                  start=(dt == 0), stop=(dt == NDT - 1), inc=0)
                            pe.op("matmul", psR2x[:, 0:64],
                                  xfT[:, dt * 1024 + tt * 128: dt * 1024 + tt * 128 + 128],
                                  at3[:, dt * 64:(dt + 1) * 64],
                                  start=(dt == 0), stop=(dt == NDT - 1),
                                  inc=(1 if dt == NDT - 1 else 0))
                        pe.mark(f"i_rt{tt}")
                        for e in range(16):
                            pe.waitm(f"i_rhe{tt}_{e}")
                            if tt * 16 + e >= 1:
                                pidx = tt * 16 + e - 1
                                pe.waitm(f"i_rhp{pidx // 16}_{pidx % 16}")
                            pe.op("transpose", psT3[0:64, 0:128], rhe[:], idbf[:], inc=1)
                            pe.mark(f"i_rtr{tt}_{e}")
                    for dt in range(NDT):
                        pe.waitm(f"bw{dt}")
                        if dt == 0:
                            pe.waitm(f"i_rhp7_15")
                        if dt >= 2:
                            pe.waitm(f"i_ob{dt-2}")
                        for d in range(NDT):
                            for tg in range(2):
                                pe.op("matmul", psO3[dt % 2][:, tg * 512:(tg + 1) * 512],
                                      bw3[:, (dt % 2) * 4096 + d * 128:(dt % 2) * 4096 + (d + 1) * 128],
                                      xfT[:, d * 1024 + tg * 512: d * 1024 + (tg + 1) * 512],
                                      start=(d == 0), stop=False, inc=0)
                        for pair in range(8):
                            for tg in range(2):
                                pe.op("matmul", psO3[dt % 2][:, tg * 512:(tg + 1) * 512],
                                      bp3[:, (dt % 2) * 1024 + pair * 128:(dt % 2) * 1024 + (pair + 1) * 128],
                                      rhp[:, pair * 1024 + tg * 512: pair * 1024 + (tg + 1) * 512],
                                      start=False, stop=(pair == 7),
                                      inc=(1 if (pair == 7 and tg == 1) else 0))
                        pe.mark(f"fmm{dt}")

                def pi_ac():
                    ac.waitm(f"xadone{NDT-1}")
                    ac.waitm(f"xaev{NDT-1}")
                    for dt in range(NDT):
                        if dt == 0:
                            ac.waitm("i_xf")
                        else:
                            ac.waitm(f"i_qmm{dt-1}")
                        ac.op("activation", sq3[:], xfT[:, dt * 1024:(dt + 1) * 1024], AF.Square)
                        ac.mark(f"i_sq{dt}")

                def pi_ac3():
                    for tt in range(8):
                        ac.waitm(f"i_nmax{tt}")
                        ac.op("activation", pr3[:], psRt[:, 0:16], AF.Exp, bias=sc3[:, 0:1])
                        ac.mark(f"i_exp{tt}")
                        ac.waitm(f"i_rt{tt}")
                        ac.op("copy", rhb[:], psR2x[:, 0:64])
                        ac.mark(f"i_rh_ev{tt}")

                def pi_dv():
                    dv.waitm(f"xaev{NDT-1}")
                    dv.waitm(f"omm{NDT-1}")
                    dv.waitm("i_sum")
                    dv.op("tensor_scalar_mul", v3a[:], psOa[:], 1.0 / DM)
                    dv.op("tensor_copy", mb3[:], v3a[:])
                    dv.mark("i_m")
                    dv.waitm("i_sumsq")
                    dv.op("tensor_scalar_mul", v3b[:], psOb[:], 1.0 / DM)
                    dv.op("tensor_mul", psOa[:], v3a[:], v3a[:]) if False else None
                    dv.op("tensor_mul", v3a[:], v3a[:], v3a[:])
                    dv.op("tensor_sub", v3b[:], v3b[:], v3a[:])
                    dv.mark("i_var")
                    dv.waitm("i_rstd")
                    dv.op("tensor_copy", rs3[:], v3b[:])
                    dv.mark("i_rs3")
                    for dt in range(NDT):
                        dv.op("tensor_sub", xfT[:, dt * 1024:(dt + 1) * 1024],
                              xfT[:, dt * 1024:(dt + 1) * 1024], mb3[:])
                        dv.op("tensor_mul", xfT[:, dt * 1024:(dt + 1) * 1024],
                              xfT[:, dt * 1024:(dt + 1) * 1024], rs3[:])
                        dv.op("drain", inc=1)
                        dv.mark(f"i_xfok{dt}")
                    for tt in range(8):
                        dv.waitm(f"i_rt{tt}")
                        dv.op("reduce_max", sc3[:, 0:1], psRt[:, 0:16], axis=AXX, negate=True)
                        dv.mark(f"i_nmax{tt}")
                        dv.waitm(f"i_exp{tt}")
                        dv.op("reduce_sum", sc3[:, 1:2], pr3[:], axis=AXX)
                        dv.op("reciprocal", sc3[:, 1:2], sc3[:, 1:2])
                        dv.op("tensor_scalar_mul", pr3[:], pr3[:], sc3[:, 1:2])
                        dv.op("reduce_max", sc3[:, 2:3], pr3[:], axis=AXX)
                        dv.op("tensor_scalar", ce3[:], pr3[:], sc3[:, 2:3], None, ALU.is_equal)
                        dv.op("scalar_tensor_tensor", v3a[:, 0:16], ce3[:], -1e30, pr3[:],
                              ALU.mult, ALU.add)
                        dv.op("reduce_max", sc3[:, 3:4], v3a[:, 0:16], axis=AXX)
                        dv.op("tensor_tensor", sc3[:, 4:5], sc3[:, 2:3], sc3[:, 3:4], ALU.add)
                        dv.op("tensor_scalar_add", sc3[:, 4:5], sc3[:, 4:5], 1e-6)
                        dv.op("reciprocal", sc3[:, 4:5], sc3[:, 4:5])
                        dv.op("tensor_scalar_mul", sc3[:, 4:5], sc3[:, 4:5], 0.5)
                        dv.op("tensor_tensor", sc3[:, 5:6], sc3[:, 2:3], sc3[:, 4:5], ALU.mult)
                        dv.op("tensor_tensor", sc3[:, 6:7], sc3[:, 3:4], sc3[:, 4:5], ALU.mult)
                        dv.op("tensor_scalar", v3a[:, 16:32], pr3[:], sc3[:, 3:4], None, ALU.is_equal)
                        dv.op("tensor_scalar_mul", ce3[:], ce3[:], sc3[:, 5:6])
                        dv.op("scalar_tensor_tensor", ce3[:], v3a[:, 16:32], sc3[:, 6:7], ce3[:],
                              ALU.mult, ALU.add)
                        dv.op("drain", inc=1)
                        dv.mark(f"i_sm{tt}")
                        for e in range(16):
                            dv.waitm(f"i_rh_ev{tt}")
                            if e >= 1:
                                dv.waitm(f"i_rtr{tt}_{e-1}")
                            dv.op("tensor_scalar_mul", rhe[:], rhb[:], ce3[:, e:e + 1])
                            dv.op("drain", inc=1)
                            dv.mark(f"i_rhe{tt}_{e}")
                            dv.waitm(f"i_rtr{tt}_{e}")
                            dv.op("tensor_copy",
                                  rhp[(e % 2) * 64:(e % 2) * 64 + 64, (e // 2) * 1024 + tt * 128:
                                      (e // 2) * 1024 + tt * 128 + 128],
                                  psT3[0:64, 0:128])
                            dv.mark(f"i_rhp{tt}_{e}")
                    for dt in range(NDT):
                        dv.waitm(f"fmm{dt}")
                        dv.waitm(f"i_xa{dt}")
                        osl = ob3[:, (dt % 2) * 1024:(dt % 2) * 1024 + 1024]
                        dv.op("tensor_scalar_add", osl, psO3[dt % 2][:], bbcol[:, dt:dt + 1])
                        dv.op("tensor_add", osl, osl, xa3[:, (dt % 2) * 1024:(dt % 2) * 1024 + 1024])
                        dv.op("drain", inc=1)
                        dv.mark(f"i_ob{dt}")

                def pi_ac2():
                    ac.waitm("i_var")
                    ac.op("activation", v3b[:], v3b[:], AF.Ln, bias=epscol[:])
                    ac.op("activation", v3b[:], v3b[:], AF.Exp, scale=-0.5)
                    ac.mark("i_rstd")

                def pi_sy():
                    sy.waitm(f"xadone{NDT-1}")
                    for dt in range(NDT):
                        if dt >= 2:
                            sy.waitm(f"i_ob{dt-2}")
                        sy.dma(xa3[:, (dt % 2) * 1024:(dt % 2) * 1024 + 1024],
                               xab[dt * 128:(dt + 1) * 128, :])
                        sy.mark(f"i_xa{dt}")
                        sy.waitm(f"i_ob{dt}")
                        sy.dma(outT[dt * 128:(dt + 1) * 128, :],
                               ob3[:, (dt % 2) * 1024:(dt % 2) * 1024 + 1024])
                        sy.mark(f"i_out{dt}")

                _run_phase(ctx, nc, [
                    ("gpsimd", gp, pi_gp),
                    ("scalar", ac, pi_ac),
                    ("scalar", ac, pi_ac2),
                    ("scalar", ac, pi_ac3),
                    ("tensor", pe, pi_pe),
                    ("vector", dv, pi_dv),
                    ("sync", sy, pi_sy),
                ])

        def fin_sy():
            if stage < 9:
                sy.dma(outT[0:128, 0:128], idf[:])
            sy.wait_queue_idle(sy)
            sy.wait_queue_idle(ac)
            sy.wait_queue_idle(gp)
            sy.op("nop", inc=0)

        _run_phase(ctx, nc, [("sync", sy, fin_sy)])

    return nc



_NC_CACHE = {}


def _get_program():
    if "nc" not in _NC_CACHE:
        _NC_CACHE["nc"] = build(stage=9, debug=False)
    return _NC_CACHE["nc"]


def kernel(**inputs):
    hp = _host_prep(inputs)
    nc = _get_program()
    shared = {k: hp[k] for k in ("win", "wout", "wbase", "wbp", "wat", "wrout", "nwcol",
                                 "cwt", "cbt", "dtb", "alog", "dpc", "gnc", "bbc", "idn",
                                 "trim")}
    in_maps = []
    for k in range(NC):
        m = dict(shared)
        m["xt"] = hp["xt"][k]
        m["gmt"] = hp["gmt"][k]
        m["mcol"] = hp["mcol"][k]
        in_maps.append(m)
    res = run_bass_kernel_spmd(nc, in_maps, core_ids=list(range(NC)))
    out = np.empty((2, 4096, DM), np.float32)
    for k in range(NC):
        b, q = k // 4, k % 4
        out[b, 1024 * q:1024 * (q + 1)] = np.asarray(res.results[k]["outT"]).T
    return out
